# version 37
# baseline (speedup 1.0000x reference)
import contextlib
import traceback
import numpy as np

import concourse.bass as bass
import concourse.mybir as mybir
import concourse.library_config as libcfg
from concourse.bass_utils import run_bass_kernel_spmd

F32 = mybir.dt.float32
I32 = mybir.dt.int32
U32 = mybir.dt.uint32
BF16 = mybir.dt.bfloat16
AF = mybir.ActivationFunctionType
ALU = mybir.AluOpType

PI = float(np.pi)
TWO_PI = float(2 * np.pi)
LN1E4 = float(np.log(10000.0))

ENGS = ("sync", "gpsimd", "vector", "scalar", "tensor")


class CFG:
    def __init__(self, TO=2048, C=512):
        self.TO = TO
        self.TT = 2 * TO
        self.NTO = TO // 128
        self.NT = self.TT // 128
        self.H = 1024
        self.NI = self.H // 128
        self.NH = 16
        self.HD = 64
        self.HP = self.NH // 2
        self.E = 8
        self.C = C
        self.NCT = C // 128
        self.F = 4096
        self.FS = 256
        self.NFS = self.F // self.FS
        self.NFT = self.FS // 128
        self.QG = min(512, TO)
        self.NQG = TO // self.QG
        self.NQT = self.QG // 128
        self.SLW = min(512, TO)
        self.NSL = TO // self.SLW


class Sem:
    __slots__ = ("h", "n", "name")


class Em:

    def __init__(self, nc, es):
        self.nc = nc
        self.es = es
        self.steps = {e: [] for e in ENGS}
        self.sems = []
        self.waited = {e: {} for e in ENGS}

    def sem(self, name):
        s = Sem()
        s.h = self.es.enter_context(self.nc.semaphore(name))
        s.n = 0
        s.name = name
        self.sems.append(s)
        return s

    def op(self, eng, meth, *args, inc=None, amt=1, **kw):
        if inc is not None:
            inc.n += amt
        h = inc.h if inc is not None else None
        site = traceback.extract_stack(limit=3)[0]
        tag = f"{meth}@{site.filename.rsplit('/',1)[-1]}:{site.lineno}"

        def f(e, meth=meth, args=args, kw=kw, h=h, amt=amt, tag=tag):
            try:
                ins = getattr(e, meth)(*args, **kw)
            except Exception as ex:
                raise RuntimeError(f"emit failed at {tag}: {ex}") from ex
            if h is not None:
                ins.then_inc(h, amt)

        self.steps[eng].append(f)
        return (inc, inc.n) if inc is not None else None

    def wait(self, eng, mark):
        if mark is None:
            return
        if isinstance(mark, list):
            for m in mark:
                self.wait(eng, m)
            return
        if isinstance(mark, Sem):
            sem, v = mark, mark.n
        else:
            sem, v = mark
        if v <= 0 or self.waited[eng].get(sem.name, 0) >= v:
            return
        self.waited[eng][sem.name] = v
        self.steps[eng].append(lambda e, h=sem.h, v=v: e.wait_ge(h, v))

    def barrier(self):
        for e in ENGS:
            for s in self.sems:
                self.wait(e, s)

    def run(self):
        with self.nc.Block() as block:
            for eng_name in ENGS:
                deco = getattr(block, eng_name)

                def body(e, steps=self.steps[eng_name]):
                    for f in steps:
                        f(e)

                deco(body)


def build_nc(cfg: CFG):
    nc = bass.Bass("TRN2", target_bir_lowering=False)
    TO, TT, NTO, NT = cfg.TO, cfg.TT, cfg.NTO, cfg.NT
    H, NI, NH, HD, HP = cfg.H, cfg.NI, cfg.NH, cfg.HD, cfg.HP
    E, C, NCT, F, FS, NFS, NFT = cfg.E, cfg.C, cfg.NCT, cfg.F, cfg.FS, cfg.NFS, cfg.NFT
    QG, NQG, NQT = cfg.QG, cfg.NQG, cfg.NQT
    SLW, NSL = cfg.SLW, cfg.NSL
    TH = TO // 2
    WW = min(512, TO)

    d_x_own = nc.dram_tensor("x_own", [TO, H], F32, kind="ExternalInput").ap()
    d_x_oth = nc.dram_tensor("x_oth", [TO, H], F32, kind="ExternalInput").ap()
    d_pos_own = nc.dram_tensor("pos_own", [128, 1], F32, kind="ExternalInput").ap()
    d_pos_oth = nc.dram_tensor("pos_oth", [128, 1], F32, kind="ExternalInput").ap()
    d_wqT = nc.dram_tensor("wqT", [128, NI * H], F32, kind="ExternalInput").ap()
    d_wkT = nc.dram_tensor("wkT", [128, HP * NI * 128], F32, kind="ExternalInput").ap()
    d_wvT = nc.dram_tensor("wvT", [128, HP * NI * 128], F32, kind="ExternalInput").ap()
    d_woT = nc.dram_tensor("woT", [128, NI * H], F32, kind="ExternalInput").ap()
    d_bq = nc.dram_tensor("bq", [128, NI], F32, kind="ExternalInput").ap()
    d_bk = nc.dram_tensor("bk", [128, NI], F32, kind="ExternalInput").ap()
    d_bv = nc.dram_tensor("bv", [1, H], F32, kind="ExternalInput").ap()
    d_bo = nc.dram_tensor("bo", [1, H], F32, kind="ExternalInput").ap()
    d_wr = nc.dram_tensor("w_router", [H, E], F32, kind="ExternalInput").ap()
    d_wi = nc.dram_tensor("wi", [E * NFS, 128, NI * FS], F32, kind="ExternalInput").ap()
    d_wo = nc.dram_tensor("wo_moe", [E * NFS, 128, NFT * H], F32, kind="ExternalInput").ap()
    d_y = nc.dram_tensor("y", [TO, H], F32, kind="ExternalOutput").ap()
    d_xnT = nc.dram_tensor("xnT", [TT // WW, 128, NI * WW], F32, kind="Internal").ap()
    d_xnh = nc.dram_tensor("xn_h", [TO, H], BF16, kind="Internal").ap()
    d_xnl = nc.dram_tensor("xn_l", [TO, H], BF16, kind="Internal").ap()
    DBG = getattr(cfg, "debug", False)
    d_ost = nc.dram_tensor("O_st", [HP, 128, TO], F32,
                           kind="ExternalOutput" if DBG else "Internal").ap()
    if DBG:
        d_dbg_probs = nc.dram_tensor("dbg_probs", [128, NTO * E], F32, kind="ExternalOutput").ap()
        d_dbg_prio = nc.dram_tensor("dbg_prio", [128, NTO * E], F32, kind="ExternalOutput").ap()
        d_dbg_gate = nc.dram_tensor("dbg_gate", [E, 128, NCT], F32, kind="ExternalOutput").ap()
        d_dbg_ei = nc.dram_tensor("dbg_ei", [128, NI * C], F32, kind="ExternalOutput").ap()
        d_dbg_eo = nc.dram_tensor("dbg_eo", [128, NCT * H], F32, kind="ExternalOutput").ap()
        d_dbg_moe = nc.dram_tensor("dbg_moe", [TO, H], F32, kind="ExternalOutput").ap()

    es = contextlib.ExitStack()
    em = Em(nc, es)

    def sb(name, shape, dt=F32, st=None):
        return (st or es).enter_context(nc.sbuf_tensor(name, shape, dt))

    PS_h = es.enter_context(nc.psum_tensor("PS", [128, 4096], F32))

    def BK(b, lo=0, n=512, p0=0, pn=128):
        return PS_h[p0:p0 + pn, 512 * b + lo: 512 * b + lo + n]

    psfree = {}

    def usebanks(*bs):
        for b in bs:
            em.wait("tensor", psfree.get(b))

    def freebanks(mark, *bs):
        for b in bs:
            psfree[b] = mark

    sv = em.sem("sv")
    ss = em.sem("ss")
    sp = em.sem("sp")
    sg = em.sem("sg")
    sW = [em.sem(f"sW{i}") for i in range(14)]
    sG = [em.sem(f"sG{i}") for i in range(10)]

    def V(meth, *a, **k):
        return em.op("vector", meth, *a, inc=sv, **k)

    def Sc(meth, *a, **k):
        return em.op("scalar", meth, *a, inc=ss, **k)

    def T(meth, *a, **k):
        return em.op("tensor", meth, *a, inc=sp, **k)

    def Tq(meth, *a, **k):
        return em.op("tensor", meth, *a, inc=None, **k)

    def G(meth, *a, **k):
        return em.op("gpsimd", meth, *a, inc=sg, **k)

    def ld(slot, out, in_):
        em.wait("sync", (sW[slot], sW[slot].n))
        return em.op("sync", "dma_start", out=out, in_=in_, inc=sW[slot], amt=16)

    def st(slot, out, in_):
        em.wait("gpsimd", (sG[slot], sG[slot].n))
        return em.op("gpsimd", "dma_start", out=out, in_=in_, inc=sG[slot], amt=16)

    def bcast(dst, srcrow, n, bank, pn=128, src_mark=None, dst_free=None):
        wt_ = em.wait
        if src_mark is not None:
            wt_("tensor", src_mark)
        mlast = None
        for o in range(0, n, 512):
            w = min(512, n - o)
            usebanks(bank)
            mm = T("matmul", BK(bank, 0, w, 0, pn), c_ones[0:1, 0:pn],
                   srcrow[0:1, o:o + w], start=True, stop=True)
            em.wait("vector", mm)
            if dst_free is not None:
                em.wait("vector", dst_free)
            mlast = V("tensor_copy", out=dst[0:pn, o:o + w], in_=BK(bank, 0, w, 0, pn))
            freebanks(mlast, bank)
        return mlast

    wv = lambda m: em.wait("vector", m)
    ws = lambda m: em.wait("scalar", m)
    wt = lambda m: em.wait("tensor", m)
    wg = lambda m: em.wait("gpsimd", m)
    wsync = lambda m: em.wait("sync", m)

    c_ident = sb("c_ident", [128, 128])
    c_iota128 = sb("c_iota128", [128, 128])
    c_rowid = sb("c_rowid", [128, 1])
    c_LT = sb("c_LT", [128, 128])
    c_ones = sb("c_ones", [128, 128])
    c_e8 = sb("c_e8", [128, E])
    c_iotaC = sb("c_iotaC", [128, C])
    c_cval = sb("c_cval", [128, NCT])
    c_ones2 = sb("c_ones2", [128, 2])
    c_iota32 = sb("c_iota32", [128, 32])
    c_invf = sb("c_invf", [128, 32])
    c_pib = sb("c_pib", [128, 1])
    c_pos = sb("c_pos", [128, 2])
    tmp_c = sb("tmp_c", [128, 4])
    tmp_i = sb("tmp_i", [128, 1], I32)

    r_probs = sb("r_probs", [128, NTO * E])
    r_probs_h = sb("r_probs_h", [128, NTO * E], BF16)
    r_probs_l = sb("r_probs_l", [128, NTO * E], BF16)
    r_prio = sb("r_prio", [128, NTO * E])
    r_qmax2 = sb("r_qmax2", [max(NQG, 2), HP])

    G("iota", c_iota128[:], pattern=[[1, 128]], base=0, channel_multiplier=0,
      allow_small_or_imprecise_dtypes=True)
    G("iota", c_rowid[:], pattern=[[0, 1]], base=0, channel_multiplier=1,
      allow_small_or_imprecise_dtypes=True)
    G("iota", c_e8[:], pattern=[[1, E]], base=0, channel_multiplier=0,
      allow_small_or_imprecise_dtypes=True)
    G("iota", c_iotaC[:], pattern=[[1, C]], base=0, channel_multiplier=0,
      allow_small_or_imprecise_dtypes=True)
    for ct in range(NCT):
        G("iota", c_cval[:, ct:ct + 1], pattern=[[0, 1]], base=ct * 128,
          channel_multiplier=1, allow_small_or_imprecise_dtypes=True)
    G("iota", c_iota32[:], pattern=[[1, 32]], base=0, channel_multiplier=0,
      allow_small_or_imprecise_dtypes=True)
    mi8 = G("iota", tmp_c[:, 0:1], pattern=[[0, 1]], base=0, channel_multiplier=1,
            allow_small_or_imprecise_dtypes=True)
    wv(mi8)
    V("memset", c_ones[:], 1.0)
    V("memset", c_pib[:], -PI)
    V("memset", r_qmax2[:], 0.0)
    V("tensor_scalar", out=c_ident[:], in0=c_iota128[:], scalar1=c_rowid[:],
      scalar2=None, op0=ALU.is_equal)
    V("tensor_scalar", out=c_LT[:], in0=c_iota128[:], scalar1=c_rowid[:],
      scalar2=None, op0=ALU.is_ge)
    V("tensor_single_scalar", c_ones2[:, 0:1], c_rowid[:], 64.0, ALU.is_lt)
    V("tensor_single_scalar", c_ones2[:, 1:2], c_rowid[:], 64.0, ALU.is_ge)
    ws(sg)
    m_invf = Sc("activation", out=c_invf[:], in_=c_iota32[:], func=AF.Exp,
                scale=-LN1E4 / 32.0)
    ld(3, c_pos[:, 0:1], d_pos_own)
    m_pos = ld(3, c_pos[:, 1:2], d_pos_oth)

    p0 = contextlib.ExitStack()
    xin = [sb(f"p0_xin{i}", [128, H], st=p0) for i in range(2)]
    xnb = [sb(f"p0_xnb{i}", [128, H], st=p0) for i in range(2)]
    WPT = WW // 128
    trsw = [sb(f"p0_trsw{i}", [128, NI * WW], st=p0) for i in range(2)]
    st6 = sb("p0_st6", [128, 12], st=p0)
    mv2 = sb("p0_mv", [128, 2], st=p0)
    lncol = sb("p0_lncol", [128, 4], st=p0)
    wr8 = sb("p0_wr8", [128, NI * E], st=p0)

    ld(3, wr8[:].rearrange("p (i e) -> p i e", i=NI),
       d_wr.rearrange("(i p) e -> p i e", p=128))
    wr_mark = (sW[3], sW[3].n)

    xh = [sb(f"p0_xh{i}", [128, H], BF16, st=p0) for i in range(2)]
    xl = [sb(f"p0_xl{i}", [128, H], BF16, st=p0) for i in range(2)]
    xin_free = [None, None]
    xnb_free = [None, None]
    trsw_free = [None, None]
    wrouter_marks = [[], []]
    xh_free = [None, None]
    xl_free = [None, None]
    for tt in range(NT):
        own = tt < NTO
        b = tt % 2
        src = d_x_own if own else d_x_oth
        row0 = (tt % NTO) * 128
        wsync(xin_free[b])
        ld(b, xin[b][:], src[row0:row0 + 128, :])
        wv((sW[b], sW[b].n))
        V("bn_stats", st6[:, 0:6], xin[b][:, 0:512])
        V("bn_stats", st6[:, 6:12], xin[b][:, 512:1024])
        wv(sv)
        V("bn_aggr", mv2[:], st6[:])
        wv(sv)
        V("tensor_single_scalar", lncol[:, 0:1], mv2[:, 1:2], 1e-5, ALU.add)
        ws(sv)
        Sc("activation", out=lncol[:, 1:2], in_=lncol[:, 0:1], func=AF.Sqrt)
        wv(ss)
        V("reciprocal", lncol[:, 2:3], lncol[:, 1:2])
        wv(sv)
        wv(xnb_free[b])
        mxn = V("tensor_scalar", out=xnb[b][:], in0=xin[b][:], scalar1=mv2[:, 0:1],
                scalar2=lncol[:, 2:3], op0=ALU.subtract, op1=ALU.mult)
        xin_free[b] = mxn
        mst = None
        if own:
            wv(sv)
            wv(xh_free[b])
            V("tensor_copy", out=xh[b][:], in_=xnb[b][:])
            wv(sv)
            mlo = V("tensor_tensor", out=xl[b][:], in0=xnb[b][:], in1=xh[b][:],
                    op=ALU.subtract)
            wg(mlo)
            msth = st(b, d_xnh[row0:row0 + 128, :], xh[b][:])
            mst = st(2 + b, d_xnl[row0:row0 + 128, :], xl[b][:])
            xh_free[b] = [msth, mst]
        wt(mxn)
        usebanks(2 * b, 2 * b + 1)
        for j in range(NI):
            mtr = T("transpose", PS_h[:, b * 1024 + j * 128: b * 1024 + j * 128 + 128],
                    xnb[b][:, j * 128:j * 128 + 128], c_ident[:])
        w_ = tt // WPT
        wb = w_ % 2
        ti = tt % WPT
        if ti == 0:
            wrouter_marks[wb] = []
        wv(mtr)
        wv(trsw_free[wb] if ti == 0 else None)
        mev = V("tensor_copy",
                out=trsw[wb][:].rearrange("p (j t) -> p j t", j=NI)[:, :, ti * 128:(ti + 1) * 128],
                in_=PS_h[:, b * 1024: b * 1024 + H].rearrange("p (j t) -> p j t", j=NI))
        freebanks(mev, 2 * b, 2 * b + 1)
        xnb_free[b] = [mev] + ([mst] if mst else [])
        if ti == WPT - 1:
            wg(mev)
            mw = st(4 + wb, d_xnT[w_], trsw[wb][:])
            trsw_free[wb] = [mw] + wrouter_marks[wb]
        if own:
            wt(wr_mark)
            wt(mev)
            usebanks(4)
            for j in range(NI):
                mm = (T if j == NI - 1 else Tq)(
                    "matmul", BK(4, 0, E),
                    trsw[wb][:, j * WW + ti * 128: j * WW + ti * 128 + 128],
                    wr8[:, j * E:(j + 1) * E], start=(j == 0), stop=(j == NI - 1))
            wrouter_marks[wb].append(mm)
            if ti == WPT - 1:
                trsw_free[wb].append(mm)
            wv(mm)
            V("reduce_max", tmp_c[:, 0:1], BK(4, 0, E), axis=mybir.AxisListType.X)
            wv(sv)
            V("tensor_single_scalar", tmp_c[:, 1:2], tmp_c[:, 0:1], -1.0, ALU.mult)
            ws(sv)
            mex = Sc("activation", out=r_probs[:, tt * E:(tt + 1) * E], in_=BK(4, 0, E),
                     func=AF.Exp, bias=tmp_c[:, 1:2], accum_out=tmp_c[:, 2:3])
            freebanks(mex, 4)
            wv(mex)
            V("reciprocal", tmp_c[:, 3:4], tmp_c[:, 2:3])
            wv(sv)
            V("tensor_scalar", out=r_probs[:, tt * E:(tt + 1) * E],
              in0=r_probs[:, tt * E:(tt + 1) * E], scalar1=tmp_c[:, 3:4],
              scalar2=None, op0=ALU.mult)

    em.barrier()
    p0.close()

    pr = contextlib.ExitStack()
    top8 = sb("pr_top8", [128, 8], st=pr)
    idxu = sb("pr_idxu", [128, 8], U32, st=pr)
    idxf = sb("pr_idxf", [128, 8], st=pr)
    idx12 = sb("pr_idx12", [128, 2 * NTO], st=pr)
    maskkt = sb("pr_mask", [128, NT * E], st=pr)
    priokt = sb("pr_priokt", [128, NT * E], st=pr)

    for tt in range(NTO):
        pv = r_probs[:, tt * E:(tt + 1) * E]
        wv(sv)
        V("max", out=top8[:], in_=pv)
        wv(sv)
        V("max_index", out=idxu[:], in_max=top8[:], in_values=pv)
        wv(sv)
        V("tensor_copy", out=idxf[:], in_=idxu[:])
        wv(sv)
        V("tensor_copy", out=idx12[:, tt:tt + 1], in_=idxf[:, 0:1])
        V("tensor_copy", out=idx12[:, NTO + tt:NTO + tt + 1], in_=idxf[:, 1:2])
    wv(sv)
    for j in range(NT):
        V("tensor_scalar", out=maskkt[:, j * E:(j + 1) * E], in0=c_e8[:],
          scalar1=idx12[:, j:j + 1], scalar2=None, op0=ALU.is_equal)
    mmask = (sv, sv.n)
    wt(mmask)
    for j in range(NT):
        usebanks(0)
        for i in range(j):
            Tq("matmul", BK(0, 0, E), c_ones[:], maskkt[:, i * E:(i + 1) * E],
               start=(i == 0), stop=False)
        mcj = T("matmul", BK(0, 0, E), c_LT[:], maskkt[:, j * E:(j + 1) * E],
                start=(j == 0), stop=True)
        wv(mcj)
        V("tensor_tensor", out=priokt[:, j * E:(j + 1) * E], in0=BK(0, 0, E),
          in1=maskkt[:, j * E:(j + 1) * E], op=ALU.mult)
        mpr_ = (sv, sv.n)
        freebanks(mpr_, 0)
        wv(sv)
        V("tensor_single_scalar", priokt[:, j * E:(j + 1) * E],
          priokt[:, j * E:(j + 1) * E], 1.0, ALU.subtract)
    wv(sv)
    for tt in range(NTO):
        V("tensor_max", out=r_prio[:, tt * E:(tt + 1) * E],
          in0=priokt[:, tt * E:(tt + 1) * E],
          in1=priokt[:, (NTO + tt) * E:(NTO + tt + 1) * E])

    wv(sv)
    V("tensor_copy", out=r_probs_h[:], in_=r_probs[:])
    wv(sv)
    V("tensor_tensor", out=r_probs_l[:], in0=r_probs[:], in1=r_probs_h[:],
      op=ALU.subtract)
    if DBG:
        wg(sv)
        st(3, d_dbg_probs, r_probs[:])
        st(3, d_dbg_prio, r_prio[:])
    em.barrier()
    pr.close()

    pa = contextlib.ExitStack()
    cos_feat = sb("pa_cosf", [128, TT], st=pa)
    ssn_feat = sb("pa_ssnf", [128, TT], st=pa)
    tcol = sb("pa_tcol", [128, 5], st=pa)
    Q_sb = sb("pa_Qh", [128, HP * TO], BF16, st=pa)
    Q_lo = sb("pa_Ql", [128, HP * TO], BF16, st=pa)
    ptab = contextlib.ExitStack()
    iota_t = sb("pa_iott", [128, TO], st=ptab)
    rbig = sb("pa_rbig", [128, TO], st=ptab)
    mskb = sb("pa_mskb", [128, TO], st=ptab)

    wv(m_pos)
    wv(m_invf)
    G("iota", iota_t[:], pattern=[[1, TO]], base=0, channel_multiplier=0,
      allow_small_or_imprecise_dtypes=True)
    mgt = G("iota", tcol[:, 0:1], pattern=[[0, 1]], base=0, channel_multiplier=1,
            allow_small_or_imprecise_dtypes=True)
    wv(mgt)
    V("tensor_single_scalar", tcol[:, 1:2], tcol[:, 0:1], 64.0, ALU.is_ge)
    wv(sv)
    V("scalar_tensor_tensor", out=tcol[:, 1:2], in0=tcol[:, 1:2], scalar=-64.0,
      in1=tcol[:, 0:1], op0=ALU.mult, op1=ALU.add)
    wv(sv)
    V("tensor_single_scalar", tcol[:, 2:3], tcol[:, 1:2], 0.5, ALU.mult)
    wv(sv)
    V("tensor_copy", out=tcol[:, 3:4], in_=tcol[:, 2:3])
    for s_ in (32.0, 16.0, 8.0, 4.0, 2.0, 1.0):
        wv(sv)
        V("tensor_single_scalar", tcol[:, 4:5], tcol[:, 3:4], s_, ALU.is_ge)
        wv(sv)
        V("scalar_tensor_tensor", out=tcol[:, 3:4], in0=tcol[:, 4:5], scalar=-s_,
          in1=tcol[:, 3:4], op0=ALU.mult, op1=ALU.add)
    wv(sv)
    V("tensor_tensor", out=tcol[:, 2:3], in0=tcol[:, 2:3], in1=tcol[:, 3:4],
      op=ALU.subtract)
    wv(sv)
    V("tensor_scalar", out=tcol[:, 3:4], in0=tcol[:, 3:4], scalar1=4.0,
      scalar2=-1.0, op0=ALU.mult, op1=ALU.add)
    ws(sv)
    m_ic = Sc("activation", out=tcol[:, 1:2], in_=tcol[:, 2:3], func=AF.Exp,
              scale=-LN1E4 / 32.0)
    nred = max(1, int(np.ceil(np.log2(max(2.0, (TT + 64) / TWO_PI)))))
    for half in range(2):
        pcol = c_pos[:, 0:1] if half == 0 else c_pos[:, 1:2]
        c0 = half * TO
        wv(m_ic)
        wv(sv)
        V("tensor_scalar", out=rbig[:], in0=iota_t[:], scalar1=pcol,
          scalar2=tcol[:, 1:2], op0=ALU.add, op1=ALU.mult)
        for kstep in range(nred - 1, -1, -1):
            s_ = TWO_PI * (2.0 ** kstep)
            wv(sv)
            V("tensor_single_scalar", mskb[:], rbig[:], s_, ALU.is_ge)
            wv(sv)
            V("scalar_tensor_tensor", out=rbig[:], in0=mskb[:], scalar=-s_,
              in1=rbig[:], op0=ALU.mult, op1=ALU.add)
        wv(sv)
        V("tensor_scalar", out=rbig[:], in0=rbig[:], scalar1=0.0,
          scalar2=TWO_PI - 1e-5, op0=ALU.max, op1=ALU.min)
        ws(sv)
        msin = Sc("activation", out=ssn_feat[:, c0:c0 + TO], in_=rbig[:],
                  func=AF.Sin, bias=c_pib[:], scale=1.0)
        wv(msin)
        V("tensor_single_scalar", rbig[:], rbig[:], PI / 2, ALU.add)
        wv(sv)
        V("tensor_single_scalar", mskb[:], rbig[:], TWO_PI, ALU.is_ge)
        wv(sv)
        V("scalar_tensor_tensor", out=rbig[:], in0=mskb[:], scalar=-TWO_PI,
          in1=rbig[:], op0=ALU.mult, op1=ALU.add)
        wv(sv)
        V("tensor_scalar", out=rbig[:], in0=rbig[:], scalar1=0.0,
          scalar2=TWO_PI - 1e-5, op0=ALU.max, op1=ALU.min)
        ws(sv)
        Sc("activation", out=cos_feat[:, c0:c0 + TO], in_=rbig[:],
           func=AF.Sin, bias=c_pib[:], scale=1.0)
        wv(ss)
        V("tensor_scalar", out=ssn_feat[:, c0:c0 + TO], in0=ssn_feat[:, c0:c0 + TO],
          scalar1=tcol[:, 3:4], scalar2=-1.0, op0=ALU.mult, op1=ALU.mult)
        V("tensor_single_scalar", cos_feat[:, c0:c0 + TO], cos_feat[:, c0:c0 + TO],
          -1.0, ALU.mult)
    mtab = (sv, sv.n)
    em.barrier()
    ptab.close()
    half_off = 0
    qshf = None

    xnw = [sb(f"pa_xnw{i}", [128, NI * 512], st=pa) for i in range(2)]
    qtmp = [sb(f"pa_qtmp{i}", [128, 512], st=pa) for i in range(2)]
    qsh_b = sb("pa_qsh", [128, 512], st=pa)
    qcs_b = sb("pa_qcs", [128, 512], st=pa)
    sq512 = qcs_b
    bqc8 = sb("pa_bqc8", [128, NI], st=pa)
    pa0 = contextlib.ExitStack()
    wqT_sb = sb("pa_wqT", [128, NI * H], st=pa0)

    mwq = []
    for sl_, q_ in zip((2, 8, 10, 12), range(4)):
        ld(sl_, wqT_sb[:, q_ * (NI * H // 4):(q_ + 1) * (NI * H // 4)],
           d_wqT[:, q_ * (NI * H // 4):(q_ + 1) * (NI * H // 4)])
        mwq.append((sW[sl_], sW[sl_].n))
    ld(3, bqc8[:], d_bq)
    mbq = (sW[3], sW[3].n)
    SWAP = [i ^ 1 for i in range(32)]

    xnw_free = [None, None]
    qn2 = sb("pa_qn2", [2, TO], st=pa0)
    qn2w = sb("pa_qn2w", [2, max(NQG, 2)], st=pa0)
    qn2t = sb("pa_qn2t", [max(NQG, 2), 2], st=pa0)

    qp_free = [None, None]
    for w in range(TO // WW):
        bw = w % 2
        wsync(xnw_free[bw])
        ld(bw, xnw[bw][:, 0:NI * WW], d_xnT[w])
        mx = (sW[bw], sW[bw].n)
        wt(mx)
        wt(mwq)
        lastq = None
        for ho in range(NI):
            bq_ = ho % 2
            usebanks(bq_)
            for j in range(NI):
                mm = (T if j == NI - 1 else Tq)(
                    "matmul", BK(bq_, 0, WW),
                    wqT_sb[:, j * H + ho * 128: j * H + ho * 128 + 128],
                    xnw[bw][:, j * WW:(j + 1) * WW],
                    start=(j == 0), stop=(j == NI - 1))
            lastq = mm
            ws(mm)
            ws(mbq)
            ws(qp_free[bq_])
            mqe = Sc("activation", out=qtmp[bq_][:, 0:WW], in_=BK(bq_, 0, WW),
                     func=AF.Identity, bias=bqc8[:, ho:ho + 1], scale=1.0)
            freebanks(mqe, bq_)
            csl = cos_feat[:, half_off + w * WW: half_off + (w + 1) * WW]
            ssl = ssn_feat[:, half_off + w * WW: half_off + (w + 1) * WW]
            wv(mqe)
            wv(qshf)
            V("stream_shuffle", out=qsh_b[:, 0:WW], in_=qtmp[bq_][:, 0:WW],
              mask=SWAP)
            V("tensor_tensor", out=qcs_b[:, 0:WW], in0=qtmp[bq_][:, 0:WW],
              in1=csl, op=ALU.mult)
            wv(sv)
            V("tensor_tensor", out=qsh_b[:, 0:WW], in0=qsh_b[:, 0:WW],
              in1=ssl, op=ALU.mult)
            wv(sv)
            V("tensor_tensor", out=qtmp[bq_][:, 0:WW], in0=qcs_b[:, 0:WW],
              in1=qsh_b[:, 0:WW], op=ALU.add)
            wv(sv)
            V("tensor_copy", out=Q_sb[:, ho * TO + w * WW: ho * TO + (w + 1) * WW],
              in_=qtmp[bq_][:, 0:WW])
            wv(sv)
            mql = V("scalar_tensor_tensor",
                    out=Q_lo[:, ho * TO + w * WW: ho * TO + (w + 1) * WW],
                    in0=Q_sb[:, ho * TO + w * WW: ho * TO + (w + 1) * WW],
                    scalar=-1.0, in1=qtmp[bq_][:, 0:WW], op0=ALU.mult, op1=ALU.add)
            qp_free[bq_] = mql
            qshf = mql
        xnw_free[bw] = [lastq]
    mQ = (sv, sv.n)

    for hp in range(HP):
        for sl in range(NSL):
            wv(mQ)
            wv(sv)
            V("tensor_tensor", out=sq512[:, 0:SLW],
              in0=Q_sb[:, hp * TO + sl * SLW: hp * TO + (sl + 1) * SLW],
              in1=Q_sb[:, hp * TO + sl * SLW: hp * TO + (sl + 1) * SLW], op=ALU.mult)
            msq = (sv, sv.n)
            wt(msq)
            usebanks(0)
            mm = T("matmul", BK(0, 0, SLW, 0, 2), c_ones2[:], sq512[:, 0:SLW],
                   start=True, stop=True)
            wv(mm)
            mev = V("tensor_copy", out=qn2[:, sl * SLW:(sl + 1) * SLW],
                    in_=BK(0, 0, SLW, 0, 2))
            freebanks(mev, 0)
        wv(sv)
        for qg in range(NQG):
            V("reduce_max", qn2w[:, qg:qg + 1], qn2[:, qg * QG:(qg + 1) * QG],
              axis=mybir.AxisListType.X)
        mqw = (sv, sv.n)
        wt(mqw)
        usebanks(1)
        mtr = T("transpose", BK(1, 0, 2, 0, NQG), qn2w[0:2, 0:NQG], c_ident[0:2, 0:2])
        wv(mtr)
        V("tensor_copy", out=qn2t[0:NQG, :], in_=BK(1, 0, 2, 0, NQG))
        mn = (sv, sv.n)
        freebanks(mn, 1)
        wv(sv)
        V("tensor_max", out=r_qmax2[0:NQG, hp:hp + 1], in0=qn2t[0:NQG, 0:1],
          in1=qn2t[0:NQG, 1:2])

    em.barrier()
    pa0.close()

    K_sb = sb("pa_Kh", [128, TT], BF16, st=pa)
    K_lo = sb("pa_Kl", [128, TT], BF16, st=pa)
    V_sb = sb("pa_V", [128, NT * 130], st=pa)
    P_sb = [sb(f"pa_P{i}", [128, 2 * QG], st=pa) for i in range(3)]
    wk_sb = sb("pa_wk", [128, NI * 128], st=pa)
    wv_sb2 = sb("pa_wv", [128, NI * 128], st=pa)
    kn2c = sb("pa_kn2c", [2, 4], st=pa)
    nb4 = sb("pa_nb4", [max(NQG, 2), 4], st=pa)
    nbcol = sb("pa_nbcol", [128, NQG], st=pa)
    nbT = sb("pa_nbT", [1, max(NQG, 2)], st=pa)
    ost_st = sb("pa_ost", [128, 2 * QG], st=pa)
    bcl_sb = sb("pa_bcl", [128, QG], st=pa)
    bkc8 = sb("pa_bkc8", [128, HP], st=pa)
    bvrow = sb("pa_bvrow", [1, 128], st=pa)
    bvc = sb("pa_bvc", [128, 128], st=pa)

    ld(4, bkc8[:], d_bk)
    mbk8 = (sW[4], sW[4].n)
    V("memset", V_sb[:].rearrange("p (t u) -> p t u", u=130)[:, :, 64:65], 1.0)
    mones = V("memset", V_sb[:].rearrange("p (t u) -> p t u", u=130)[:, :, 129:130], 1.0)

    def ld_wkv(hp_):
        ld(2, wk_sb[:], d_wkT[:, hp_ * NI * 128:(hp_ + 1) * NI * 128])
        ld(3, wv_sb2[:], d_wvT[:, hp_ * NI * 128:(hp_ + 1) * NI * 128])
        return [(sW[2], sW[2].n), (sW[3], sW[3].n)]

    P_free = [None, None, None]
    O_free = None
    pending_O = [None, None]
    hp_of_qg = {}
    prod_free = None
    kp_free = [None, None]
    mwkv_n = ld_wkv(0)
    for hp in range(HP):
        if pending_O[0] is not None:
            O_free = evac_O(pending_O[0], pending_O[1])
            pending_O[0] = None
        attn_prev = (sp, sp.n)
        sc_prev = (ss, ss.n)
        hb = hp % 2
        mwkv = mwkv_n
        wsync(prod_free)
        wsync(mbv if hp else None)
        ld(5, bvrow[:], d_bv[:, hp * 128:(hp + 1) * 128])
        mbr = (sW[5], sW[5].n)
        wt(mbr)
        mbv = bcast(bvc, bvrow[:], 128, 5, dst_free=prod_free)
        kn_mark = None
        NWV = WW // 128
        for w in range(TT // WW):
            bw = w % 2
            wsync(xnw_free[bw])
            ld(bw, xnw[bw][:, 0:NI * WW], d_xnT[w])
            mx = (sW[bw], sW[bw].n)
            wt(mx)
            wt(mwkv)
            bk_ = 4 + bw
            usebanks(bk_)
            for j in range(NI):
                mmk = (T if j == NI - 1 else Tq)(
                    "matmul", BK(bk_, 0, WW),
                    wk_sb[:, j * 128:(j + 1) * 128],
                    xnw[bw][:, j * WW:(j + 1) * WW],
                    start=(j == 0), stop=(j == NI - 1))
            ws(mmk)
            ws(mbk8)
            ws(kp_free[bw])
            mke = Sc("activation", out=qtmp[bw][:, 0:WW], in_=BK(bk_, 0, WW),
                     func=AF.Identity, bias=bkc8[:, hp:hp + 1], scale=1.0)
            freebanks(mke, bk_)
            csl = cos_feat[:, w * WW:(w + 1) * WW]
            ssl = ssn_feat[:, w * WW:(w + 1) * WW]
            wv(attn_prev)
            wv(mke)
            wv(qshf)
            V("stream_shuffle", out=qsh_b[:, 0:WW], in_=qtmp[bw][:, 0:WW],
              mask=SWAP)
            V("tensor_tensor", out=qcs_b[:, 0:WW], in0=qtmp[bw][:, 0:WW],
              in1=csl, op=ALU.mult)
            wv(sv)
            V("tensor_tensor", out=qsh_b[:, 0:WW], in0=qsh_b[:, 0:WW],
              in1=ssl, op=ALU.mult)
            wv(sv)
            V("tensor_tensor", out=qtmp[bw][:, 0:WW], in0=qcs_b[:, 0:WW],
              in1=qsh_b[:, 0:WW], op=ALU.add)
            wv(sv)
            V("tensor_copy", out=K_sb[:, w * WW:(w + 1) * WW],
              in_=qtmp[bw][:, 0:WW])
            wv(sv)
            mkl = V("scalar_tensor_tensor", out=K_lo[:, w * WW:(w + 1) * WW],
                    in0=K_sb[:, w * WW:(w + 1) * WW], scalar=-1.0,
                    in1=qtmp[bw][:, 0:WW], op0=ALU.mult, op1=ALU.add)
            kp_free[bw] = mkl
            qshf = mkl
            for tv in range(NWV):
                bv_ = 6 + (tv % 2)
                usebanks(bv_)
                for j in range(NI):
                    mmv = (T if j == NI - 1 else Tq)(
                        "matmul", BK(bv_, 0, 128),
                        xnw[bw][:, j * WW + tv * 128: j * WW + tv * 128 + 128],
                        wv_sb2[:, j * 128:(j + 1) * 128],
                        start=(j == 0), stop=(j == NI - 1))
                tt_ = w * NWV + tv
                wv(mmv)
                wv(mbv)
                wv(mones)
                V("tensor_tensor", out=V_sb[:, tt_ * 130: tt_ * 130 + 64],
                  in0=BK(bv_, 0, 64), in1=bvc[:, 0:64], op=ALU.add)
                mvev = V("tensor_tensor", out=V_sb[:, tt_ * 130 + 65: tt_ * 130 + 129],
                         in0=BK(bv_, 64, 64), in1=bvc[:, 64:128], op=ALU.add)
                freebanks(mvev, bv_)
            xnw_free[bw] = [mmv]
            wv(sv)
            V("tensor_tensor", out=sq512[:, 0:WW], in0=K_sb[:, w * WW:(w + 1) * WW],
              in1=K_sb[:, w * WW:(w + 1) * WW], op=ALU.mult)
            msq = (sv, sv.n)
            wt(msq)
            usebanks(7)
            mm = T("matmul", BK(7, 0, WW, 0, 2), c_ones2[:], sq512[:, 0:WW],
                   start=True, stop=True)
            wv(mm)
            V("reduce_max", kn2c[:, 1:2], BK(7, 0, WW, 0, 2),
              axis=mybir.AxisListType.X)
            mrm = (sv, sv.n)
            freebanks(mrm, 7)
            wv(sv)
            if w == 0:
                kn_mark = V("tensor_copy", out=kn2c[:, 0:1], in_=kn2c[:, 1:2])
            else:
                kn_mark = V("tensor_max", out=kn2c[:, 0:1], in0=kn2c[:, 0:1],
                            in1=kn2c[:, 1:2])
                prod_free = kn_mark
        if hp + 1 < HP:
            wsync((sp, sp.n))
            mwkv_n = ld_wkv(hp + 1)
        wt(kn_mark)
        usebanks(4)
        mtr = T("transpose", BK(4, 0, 2, 0, 1), kn2c[0:2, 0:1], c_ident[0:2, 0:2])
        wv(mtr)
        V("reduce_max", kn2c[0:1, 2:3], BK(4, 0, 2, 0, 1), axis=mybir.AxisListType.X)
        mkm = (sv, sv.n)
        freebanks(mkm, 4)
        wv(mkm)
        mkb4 = bcast(nb4[:, 1:2], kn2c[0:1, 2:3], 1, 5, pn=max(NQG, 2),
                     src_mark=mkm)
        wv(mkb4)
        V("tensor_tensor", out=nb4[:, 0:1], in0=r_qmax2[:, hp:hp + 1],
          in1=nb4[:, 1:2], op=ALU.mult)
        ws(sv)
        Sc("activation", out=nb4[:, 2:3], in_=nb4[:, 0:1], func=AF.Sqrt)
        wv(ss)
        mnb = V("tensor_single_scalar", nb4[:, 3:4], nb4[:, 2:3], -0.125, ALU.mult)
        wt(mnb)
        usebanks(4)
        mtr2 = T("transpose", BK(4, 0, max(NQG, 2), 0, 1), nb4[:, 3:4],
                 c_ident[0:max(NQG, 2), 0:max(NQG, 2)])
        wv(mtr2)
        mnbt = V("tensor_copy", out=nbT[0:1, :], in_=BK(4, 0, max(NQG, 2), 0, 1))
        freebanks(mnbt, 4)
        for qg in range(NQG):
            mbc = bcast(nbcol[:, qg:qg + 1], nbT[:, qg:qg + 1], 1, 5,
                        src_mark=mnbt, dst_free=sc_prev)
        def evac_O(qg, mo_):
            wv(mo_)
            wv((sG[0], sG[0].n))
            for h in range(2):
                wv(sv)
                V("reciprocal", bcl_sb[64:65, 0:QG], BK(4 + h, 0, QG, 64, 1))
                wv(sv)
                mrc = (sv, sv.n)
                wt(mrc)
                usebanks(6)
                mbl = T("matmul", BK(6, 0, QG, 0, 64), c_ones[64:65, 0:64],
                        bcl_sb[64:65, 0:QG], start=True, stop=True)
                wv(mbl)
                V("tensor_copy", out=bcl_sb[0:64, 0:QG], in_=BK(6, 0, QG, 0, 64))
                mcl = (sv, sv.n)
                freebanks(mcl, 6)
                wv(sv)
                mev = V("tensor_tensor", out=ost_st[0:64, h * QG:(h + 1) * QG],
                        in0=BK(4 + h, 0, QG, 0, 64), in1=bcl_sb[0:64, 0:QG],
                        op=ALU.mult)
            freebanks(mev, 4, 5)
            wg(mev)
            for h in range(2):
                st(0, d_ost[hp_of_qg[qg], h * 64:(h + 1) * 64, qg * QG:(qg + 1) * QG],
                   ost_st[0:64, h * QG:(h + 1) * QG])
            return mev

        for qg in range(NQG):
            mex = {}
            for kt in range(NT + 2):
                if kt < NT:
                    sbuf_i = kt % 3
                    b0 = 2 * (kt % 2)
                    usebanks(b0, b0 + 1)
                    qsl = slice(hp * TO + qg * QG, hp * TO + (qg + 1) * QG)
                    ksl = slice(kt * 128, (kt + 1) * 128)
                    onebank = (2 * QG <= 512)
                    for pi_, (kk_, qq_) in enumerate(
                            ((K_sb, Q_sb), (K_sb, Q_lo), (K_lo, Q_sb))):
                        ma = (T if pi_ == 2 else Tq)(
                            "matmul", PS_h[:, b0 * 512: b0 * 512 + QG],
                            kk_[0:64, ksl], qq_[0:64, qsl],
                            start=(pi_ == 0),
                            stop=(pi_ == 2 and not onebank), tile_position=(0, 0))
                        mb = (T if pi_ == 2 else Tq)(
                            "matmul", PS_h[:, b0 * 512 + QG: b0 * 512 + 2 * QG],
                            kk_[64:128, ksl], qq_[64:128, qsl],
                            start=(pi_ == 0 and not onebank), stop=(pi_ == 2),
                            tile_position=(64, 0))
                    ws([ma, mb, mbc])
                    ws(P_free[sbuf_i])
                    mex[kt] = Sc("activation", out=P_sb[sbuf_i][:],
                                 in_=PS_h[:, b0 * 512: b0 * 512 + 2 * QG], func=AF.Exp,
                                 scale=0.125, bias=nbcol[:, qg:qg + 1])
                    freebanks(mex[kt], b0, b0 + 1)
                if kt == 1 and pending_O[0] is not None:
                    O_free = evac_O(pending_O[0], pending_O[1])
                    pending_O[0] = None
                if kt >= 2:
                    kp = kt - 2
                    pi = kp % 3
                    wt(mex[kp])
                    if kp == 0:
                        wt(O_free)
                        usebanks(4, 5)
                    for h in range(2):
                        mo = (T if (h == 1 or kp == NT - 1) else Tq)(
                            "matmul", BK(4 + h, 0, QG, 0, 65),
                            V_sb[:, kp * 130 + h * 65: kp * 130 + (h + 1) * 65],
                            P_sb[pi][:, h * QG: h * QG + QG],
                            start=(kp == 0), stop=(kp == NT - 1))
                    P_free[pi] = mo
            hp_of_qg[qg] = hp
            pending_O[0] = qg
            pending_O[1] = mo

    if pending_O[0] is not None:
        O_free = evac_O(pending_O[0], pending_O[1])
        pending_O[0] = None
    em.barrier()
    pa.close()

    pm = contextlib.ExitStack()
    out_acc = sb("pm_out", [128, NTO * H], st=pm)
    pmm = contextlib.ExitStack()
    xnstr = [[sb(f"pm_xn{i}{p}", [128, H], BF16, st=pmm) for i in range(3)]
             for p in range(2)]
    dispb = [sb(f"pm_disp{i}", [128, C], BF16, st=pmm) for i in range(2)]
    ei_h = sb("pm_eih", [128, NI * C], BF16, st=pmm)
    ei_l = sb("pm_eil", [128, NI * C], BF16, st=pmm)
    hact_f = sb("pm_hactf", [128, NFT * C], st=pmm)
    hact_h = [sb(f"pm_hacth{i}", [128, NFT * C], BF16, st=pmm) for i in range(2)]
    hact_l = [sb(f"pm_hactl{i}", [128, NFT * C], BF16, st=pmm) for i in range(2)]
    eo_sb = sb("pm_eo", [128, NCT * H], st=pmm)
    eo_h = sb("pm_eoh", [128, NCT * H], BF16, st=pmm)
    eo_l = sb("pm_eol", [128, NCT * H], BF16, st=pmm)
    gate_sb = sb("pm_gate", [128, NCT], st=pmm)
    wi_h = [sb(f"pm_wih{i}", [128, NI * FS], BF16, st=pmm) for i in range(2)]
    wi_l = [sb(f"pm_wil{i}", [128, NI * FS], BF16, st=pmm) for i in range(2)]
    wo_h = [sb(f"pm_woh{i}", [128, NFT * H], BF16, st=pmm) for i in range(2)]
    wo_l = [sb(f"pm_wol{i}", [128, NFT * H], BF16, st=pmm) for i in range(2)]
    wfi = [sb(f"pm_wfi{i}", [128, NI * FS], st=pmm) for i in range(2)]
    dispTg = [sb(f"pm_dispTg{i}", [128, NCT * 128], BF16, st=pmm) for i in range(2)]
    c_ident_bf = sb("pm_identbf", [128, 128], BF16, st=pmm)
    midbf = V("tensor_copy", out=c_ident_bf[:], in_=c_ident[:])

    out_mark = V("memset", out_acc[:], 0.0)

    xn_free = [None, None, None]
    disp_free = [None, None]
    wi_free = [None, None]
    wo_free = [None, None]
    wfi_free = [None, None]
    wih_free = [None, None]
    wil_free = [None, None]
    woh_free = [None, None]
    wol_free = [None, None]
    hact_rd = [None, None]
    hact_f_rd = [None, None]
    dTg_rd = [None, None]
    ei_free = None
    gate_free = None
    for e in range(E):
        usebanks(0)
        for tt in range(NTO):
            b = tt % 2
            wv(disp_free[b])
            wv(sv)
            mdg = V("tensor_scalar", out=dispb[b][:], in0=c_iotaC[:],
                    scalar1=r_prio[:, tt * E + e: tt * E + e + 1], scalar2=None,
                    op0=ALU.is_equal)
            wt(mdg)
            for ct in range(NCT):
                for ph, pr_ in ((0, r_probs_h), (1, r_probs_l)):
                    mgm = (T if (ct == NCT - 1 or tt == NTO - 1) else Tq)(
                        "matmul", BK(0, ct, 1), dispb[b][:, ct * 128:(ct + 1) * 128],
                        pr_[:, tt * E + e: tt * E + e + 1],
                        start=(tt == 0 and ct == 0 and ph == 0),
                        stop=(tt == NTO - 1 and ct == NCT - 1 and ph == 1))
            disp_free[b] = mgm
        wv(mgm)
        mgev = V("tensor_copy", out=gate_sb[:], in_=BK(0, 0, NCT))
        gate_free = mgev
        freebanks(mgev, 0)
        if DBG:
            wg(mgev)
            st(3, d_dbg_gate[e], gate_sb[:])
        usebanks(*range(NI))

        def ld_xn(tt):
            b3 = tt % 3
            wsync(xn_free[b3])
            ld(b3 % 2, xnstr[0][b3][:], d_xnh[tt * 128:(tt + 1) * 128, :])
            ld(6 + b3 % 2, xnstr[1][b3][:], d_xnl[tt * 128:(tt + 1) * 128, :])
            return [(sW[b3 % 2], sW[b3 % 2].n), (sW[6 + b3 % 2], sW[6 + b3 % 2].n)]

        mxs = {0: ld_xn(0), 1: ld_xn(1)}
        for tt in range(NTO):
            b = tt % 2
            b3 = tt % 3
            if tt + 2 < NTO:
                mxs[tt + 2] = ld_xn(tt + 2)
            wv(disp_free[b])
            wv(sv)
            mdg = V("tensor_scalar", out=dispb[b][:], in0=c_iotaC[:],
                    scalar1=r_prio[:, tt * E + e: tt * E + e + 1], scalar2=None,
                    op0=ALU.is_equal)
            wt(mdg)
            wt(mxs[tt])
            for ph in range(2):
                for it in range(NI):
                    mdm = (T if (it == NI - 1 or tt == NTO - 1) else Tq)(
                        "matmul", BK(it, 0, C), xnstr[ph][b3][:, it * 128:(it + 1) * 128],
                        dispb[b][:], start=(tt == 0 and ph == 0),
                        stop=(tt == NTO - 1 and ph == 1))
            xn_free[b3] = mdm
            disp_free[b] = mdm
        wv(mdm)
        for it in range(NI):
            V("tensor_copy", out=ei_h[:, it * C:(it + 1) * C], in_=BK(it, 0, C))
            wv(sv)
            mei = V("scalar_tensor_tensor", out=ei_l[:, it * C:(it + 1) * C],
                    in0=ei_h[:, it * C:(it + 1) * C], scalar=-1.0,
                    in1=BK(it, 0, C), op0=ALU.mult, op1=ALU.add)
        ei_free = mei
        freebanks(mei, *range(NI))
        if DBG and e == 0:
            wg(mei)
            st(3, d_dbg_ei, ei_sb[:])
        def ld_strip(s):
            b = s % 2
            W4 = NI * FS // 4
            W4o = NFT * H // 4
            wsync(wi_free[b])
            wsync(wfi_free[b])
            mldi = []
            for sl_, q_ in zip((2, 8, 10, 12), range(4)):
                ld(sl_, wfi[b][:, q_ * W4:(q_ + 1) * W4],
                   d_wi[e * NFS + s][:, q_ * W4:(q_ + 1) * W4])
                mldi.append((sW[sl_], sW[sl_].n))
            wv(mldi)
            wv(wi_free[b])
            V("tensor_copy", out=wi_h[b][:], in_=wfi[b][:])
            wv(sv)
            mspi = V("scalar_tensor_tensor", out=wi_l[b][:], in0=wi_h[b][:],
                     scalar=-1.0, in1=wfi[b][:], op0=ALU.mult, op1=ALU.add)
            wsync(wo_free[b])
            wsync(mspi)
            mldo = []
            for sl_, q_ in zip((3, 9, 11, 13), range(4)):
                ld(sl_, wfi[b][:, q_ * W4o:(q_ + 1) * W4o],
                   d_wo[e * NFS + s][:, q_ * W4o:(q_ + 1) * W4o])
                mldo.append((sW[sl_], sW[sl_].n))
            wv(mldo)
            wv(wo_free[b])
            V("tensor_copy", out=wo_h[b][:], in_=wfi[b][:])
            wv(sv)
            mspo = V("scalar_tensor_tensor", out=wo_l[b][:], in0=wo_h[b][:],
                     scalar=-1.0, in1=wfi[b][:], op0=ALU.mult, op1=ALU.add)
            wfi_free[b] = mspo
            return [mspi, mspo]

        def mm1_strip(s, wmark):
            b = s % 2
            bb = 0 if s % 2 == 0 else 4
            wt(wmark[0])
            wt(ei_free)
            out = []
            for ft in range(NFT):
                usebanks(bb + ft)
                first = True
                for wih, eis in ((wi_h[b], ei_h), (wi_h[b], ei_l), (wi_l[b], ei_h)):
                    for it in range(NI):
                        last = (wih is wi_l[b]) and it == NI - 1
                        m1 = (T if last else Tq)(
                            "matmul", BK(bb + ft, 0, C),
                            wih[:, it * FS + ft * 128: it * FS + ft * 128 + 128],
                            eis[:, it * C:(it + 1) * C],
                            start=first, stop=last)
                        first = False
                out.append(m1)
            wi_free[b] = m1
            wil_free[b] = m1
            wih_free[b] = m1
            return out

        wmarks = {0: ld_strip(0), 1: ld_strip(1)}
        mm1m = {0: mm1_strip(0, wmarks[0])}
        for s in range(NFS):
            b = s % 2
            bb = 0 if s % 2 == 0 else 4
            for ft in range(NFT):
                ws(mm1m[s][ft])
                ws(hact_rd[s % 2])
                ws(hact_f_rd[0])
                Sc("activation", out=hact_f[:, ft * C:(ft + 1) * C],
                   in_=BK(bb + ft, 0, C), func=AF.Gelu_apprx_tanh)
                mgl = Sc("activation", out=hact_h[s % 2][:, ft * C:(ft + 1) * C],
                         in_=BK(bb + ft, 0, C), func=AF.Gelu_apprx_tanh)
                freebanks(mgl, bb + ft)
            wv(mgl)
            wv(sv)
            mhl = V("scalar_tensor_tensor", out=hact_l[s % 2][:],
                    in0=hact_h[s % 2][:], scalar=-1.0, in1=hact_f[:],
                    op0=ALU.mult, op1=ALU.add)
            hact_f_rd[0] = mhl
            if s + 1 < NFS:
                mm1m[s + 1] = mm1_strip(s + 1, wmarks[s + 1])
            wt(mhl)
            wt(wmarks[s][1])
            for ct in range(NCT):
                for hh in range(2):
                    b2 = (2, 3, 6, 7)[(ct * 2 + hh) % 4]
                    usebanks(b2)
                    first = True
                    for ha, wos in ((hact_h[s % 2], wo_h[b]), (hact_h[s % 2], wo_l[b]),
                                    (hact_l[s % 2], wo_h[b])):
                        for ft in range(NFT):
                            last = (ha is hact_l[s % 2]) and ft == NFT - 1
                            mm2 = (T if (last or (ct == NCT - 1 and hh == 1 and ft == NFT - 1)) else Tq)(
                                "matmul", BK(b2, 0, 512),
                                ha[:, ft * C + ct * 128: ft * C + ct * 128 + 128],
                                wos[:, ft * H + hh * 512: ft * H + hh * 512 + 512],
                                start=(first), stop=last)
                            first = False
                    wv(mm2)
                    wv(sv)
                    if s == 0:
                        meo = V("tensor_copy",
                                out=eo_sb[:, ct * H + hh * 512: ct * H + hh * 512 + 512],
                                in_=BK(b2, 0, 512))
                    else:
                        meo = V("tensor_tensor",
                                out=eo_sb[:, ct * H + hh * 512: ct * H + hh * 512 + 512],
                                in0=eo_sb[:, ct * H + hh * 512: ct * H + hh * 512 + 512],
                                in1=BK(b2, 0, 512), op=ALU.add)
                    freebanks(meo, b2)
            hact_rd[s % 2] = mm2
            wo_free[b] = mm2
            wol_free[b] = mm2
            woh_free[b] = mm2
            if s + 2 < NFS:
                wmarks[s + 2] = ld_strip(s + 2)
        if DBG and e == 0:
            wg(meo)
            st(3, d_dbg_eo, eo_sb[:])
        wv([meo, gate_free, sv])
        for ct in range(NCT):
            V("tensor_scalar", out=eo_sb[:, ct * H:(ct + 1) * H],
              in0=eo_sb[:, ct * H:(ct + 1) * H],
              scalar1=gate_sb[:, ct:ct + 1], scalar2=None, op0=ALU.mult)
        wv(sv)
        V("tensor_copy", out=eo_h[:], in_=eo_sb[:])
        wv(sv)
        meog = V("scalar_tensor_tensor", out=eo_l[:], in0=eo_h[:], scalar=-1.0,
                 in1=eo_sb[:], op0=ALU.mult, op1=ALU.add)
        eo_free = meog
        wv(midbf)

        def gen_dispT(tt):
            b = tt % 2
            wv(disp_free[b])
            wv(sv)
            mdg = V("tensor_scalar", out=dispb[b][:], in0=c_iotaC[:],
                    scalar1=r_prio[:, tt * E + e: tt * E + e + 1], scalar2=None,
                    op0=ALU.is_equal)
            wt(mdg)
            for ct in range(NCT):
                b3 = 6 + ct % 2
                usebanks(b3)
                mtr3 = T("transpose", BK(b3, 0, 128).bitcast(BF16)[:, 0:128],
                         dispb[b][:, ct * 128:(ct + 1) * 128], c_ident_bf[:])
                wv(mtr3)
                wv(dTg_rd[b])
                mdt = V("tensor_copy", out=dispTg[b][:, ct * 128:(ct + 1) * 128],
                        in_=BK(b3, 0, 128).bitcast(BF16)[:, 0:128])
                freebanks(mdt, b3)
            disp_free[b] = mtr3
            return mdt

        mdt_t = {0: gen_dispT(0)}
        for tt in range(NTO):
            b = tt % 2
            if tt + 1 < NTO:
                mdt_t[tt + 1] = gen_dispT(tt + 1)
            wt(mdt_t[tt])
            wt(eo_free)
            for hh in range(2):
                b2 = 2 + (tt * 2 + hh) % 4
                usebanks(b2)
                first = True
                for eos in (eo_h, eo_l):
                    for ct in range(NCT):
                        last = (eos is eo_l) and ct == NCT - 1
                        mcb = (T if last else Tq)(
                            "matmul", BK(b2, 0, 512),
                            dispTg[b][:, ct * 128:(ct + 1) * 128],
                            eos[:, ct * H + hh * 512: ct * H + hh * 512 + 512],
                            start=first, stop=last)
                        first = False
                wv(mcb)
                wv(out_mark)
                out_mark = V("scalar_tensor_tensor",
                             out=out_acc[:, tt * H + hh * 512: tt * H + hh * 512 + 512],
                             in0=BK(b2, 0, 512), scalar=1.0,
                             in1=out_acc[:, tt * H + hh * 512: tt * H + hh * 512 + 512],
                             op0=ALU.mult, op1=ALU.add)
                freebanks(out_mark, b2)
            dTg_rd[b] = mcb
    if DBG:
        wg(out_mark)
        for tt in range(NTO):
            st(3, d_dbg_moe[tt * 128:(tt + 1) * 128, :], out_acc[:, tt * H:(tt + 1) * H])
    em.barrier()
    pmm.close()

    woT_sb = sb("pp_woT", [128, NI * H], st=pm)
    obuf = [sb(f"pp_ob{i}", [128, H], st=pm) for i in range(2)]

    ld(2, woT_sb[:], d_woT)
    mwo2 = (sW[2], sW[2].n)
    ob_free = [None, None]
    for tt in range(NTO):
        b = tt % 2
        wsync(ob_free[b])
        mo = []
        for hp in range(HP):
            mo.append(ld(4 + (hp % 8), obuf[b][:, hp * 128:(hp + 1) * 128],
                         d_ost[hp, :, tt * 128:(tt + 1) * 128]))
        wt(mo)
        wt(mwo2)
        for no in range(2):
            b2 = 4 + (tt * 2 + no) % 2
            usebanks(b2)
            for hp in range(HP):
                mm = (T if hp == HP - 1 else Tq)(
                    "matmul", BK(b2, 0, 512), obuf[b][:, hp * 128:(hp + 1) * 128],
                    woT_sb[:, hp * H + no * 512: hp * H + no * 512 + 512],
                    start=(hp == 0), stop=(hp == HP - 1))
            ob_free[b] = mm
            wv(mm)
            wv(out_mark)
            out_mark = V("scalar_tensor_tensor",
                         out=out_acc[:, tt * H + no * 512: tt * H + no * 512 + 512],
                         in0=BK(b2, 0, 512), scalar=1.0,
                         in1=out_acc[:, tt * H + no * 512: tt * H + no * 512 + 512],
                         op0=ALU.mult, op1=ALU.add)
            freebanks(out_mark, b2)

    borow = sb("pf_borow", [1, H], st=pm)
    bob = sb("pf_bob", [128, H], st=pm)
    xfin = [sb(f"pf_x{i}", [128, H], st=pm) for i in range(2)]
    ld(3, borow[:], d_bo)
    wt((sW[3], sW[3].n))
    mbo = bcast(bob, borow, H, 6)
    xf_free = [None, None]
    for tt in range(NTO):
        b = tt % 2
        wsync(xf_free[b])
        mx = ld(b, xfin[b][:], d_x_own[tt * 128:(tt + 1) * 128, :])
        wv([mx, mbo, out_mark])
        V("tensor_tensor", out=out_acc[:, tt * H:(tt + 1) * H],
          in0=out_acc[:, tt * H:(tt + 1) * H], in1=bob[:], op=ALU.add)
        wv(sv)
        my = V("tensor_tensor", out=out_acc[:, tt * H:(tt + 1) * H],
               in0=out_acc[:, tt * H:(tt + 1) * H], in1=xfin[b][:], op=ALU.add)
        xf_free[b] = my
        wg(my)
        st(6, d_y[tt * 128:(tt + 1) * 128, :], out_acc[:, tt * H:(tt + 1) * H])
    em.wait("gpsimd", sG[6])
    em.barrier()
    pm.close()

    em.run()
    es.close()
    return nc


_CFG = CFG()


def _rope_perm(nh=16, hd=64):
    p = []
    for h in range(nh):
        for j in range(hd // 2):
            p.append(h * hd + j)
            p.append(h * hd + j + hd // 2)
    return np.array(p)


def _pit(m):
    r, c = m.shape
    return np.ascontiguousarray(m.reshape(r // 128, 128, c).transpose(1, 0, 2)
                                .reshape(128, -1))


def _wi3(w, cfg):
    E, H, F = w.shape
    NI, NFS, FS = H // 128, F // cfg.FS, cfg.FS
    return np.ascontiguousarray(w.reshape(E, NI, 128, NFS, FS)
                                .transpose(0, 3, 2, 1, 4).reshape(E * NFS, 128, NI * FS))


def _wo3(w, cfg):
    E, F, H = w.shape
    NFS, FS = F // cfg.FS, cfg.FS
    NFT = FS // 128
    return np.ascontiguousarray(w.reshape(E, NFS, NFT, 128, H)
                                .transpose(0, 1, 3, 2, 4).reshape(E * NFS, 128, NFT * H))


def _phpmaj(m):
    r, c = m.shape
    return np.ascontiguousarray(m.reshape(r // 128, 128, c // 128, 128)
                                .transpose(1, 2, 0, 3).reshape(128, -1))


def kernel(L0_attn_x, L0_attn_wq, L0_attn_bq, L0_attn_wk, L0_attn_bk,
           L0_attn_wv, L0_attn_bv, L0_attn_wo, L0_attn_bo,
           L0_ffn_w_router, L0_ffn_wi, L0_ffn_wo):
    cfg = _CFG
    perm = _rope_perm(cfg.NH, cfg.HD)
    x = np.ascontiguousarray(np.asarray(L0_attn_x, dtype=np.float32))
    Bb, S, H = x.shape
    TO = cfg.TO
    asc = np.ascontiguousarray
    shared = {
        "wqT": _pit(np.asarray(L0_attn_wq, np.float32)[perm].T),
        "wkT": _phpmaj(np.asarray(L0_attn_wk, np.float32)[perm].T),
        "wvT": _phpmaj(np.asarray(L0_attn_wv, np.float32).T),
        "woT": _pit(np.asarray(L0_attn_wo, np.float32).T),
        "bq": asc(np.asarray(L0_attn_bq, np.float32)[perm].reshape(-1, 128).T),
        "bk": asc(np.asarray(L0_attn_bk, np.float32)[perm].reshape(-1, 128).T),
        "bv": asc(np.asarray(L0_attn_bv, np.float32).reshape(1, H)),
        "bo": asc(np.asarray(L0_attn_bo, np.float32).reshape(1, H)),
        "w_router": asc(np.asarray(L0_ffn_w_router, np.float32)),
        "wi": _wi3(np.asarray(L0_ffn_wi, np.float32), cfg),
        "wo_moe": _wo3(np.asarray(L0_ffn_wo, np.float32), cfg),
    }
    in_maps = []
    for core in range(8):
        b, half = core // 2, core % 2
        m = dict(shared)
        m["x_own"] = asc(x[b, half * TO:(half + 1) * TO, :])
        m["x_oth"] = asc(x[b, (1 - half) * TO:(2 - half) * TO, :])
        m["pos_own"] = np.full((128, 1), half * TO, np.float32)
        m["pos_oth"] = np.full((128, 1), (1 - half) * TO, np.float32)
        in_maps.append(m)
    nc = build_nc(cfg)
    res = run_bass_kernel_spmd(nc, in_maps, core_ids=list(range(8)))
    out = np.empty((Bb, S, H), np.float32)
    for core in range(8):
        b, half = core // 2, core % 2
        out[b, half * TO:(half + 1) * TO, :] = res.results[core]["y"]
    return out


# version 38 (final)
# speedup vs baseline: 1.0385x; 1.0385x over previous
import contextlib
import traceback
import numpy as np

import concourse.bass as bass
import concourse.mybir as mybir
import concourse.library_config as libcfg
from concourse.bass_utils import run_bass_kernel_spmd

F32 = mybir.dt.float32
I32 = mybir.dt.int32
U32 = mybir.dt.uint32
BF16 = mybir.dt.bfloat16
AF = mybir.ActivationFunctionType
ALU = mybir.AluOpType

PI = float(np.pi)
TWO_PI = float(2 * np.pi)
LN1E4 = float(np.log(10000.0))

ENGS = ("sync", "gpsimd", "vector", "scalar", "tensor")


class CFG:
    def __init__(self, TO=2048, C=512):
        self.TO = TO
        self.TT = 2 * TO
        self.NTO = TO // 128
        self.NT = self.TT // 128
        self.H = 1024
        self.NI = self.H // 128
        self.NH = 16
        self.HD = 64
        self.HP = self.NH // 2
        self.E = 8
        self.C = C
        self.NCT = C // 128
        self.F = 4096
        self.FS = 256
        self.NFS = self.F // self.FS
        self.NFT = self.FS // 128
        self.QG = min(512, TO)
        self.NQG = TO // self.QG
        self.NQT = self.QG // 128
        self.SLW = min(512, TO)
        self.NSL = TO // self.SLW


class Sem:
    __slots__ = ("h", "n", "name")


class Em:

    def __init__(self, nc, es):
        self.nc = nc
        self.es = es
        self.steps = {e: [] for e in ENGS}
        self.sems = []
        self.waited = {e: {} for e in ENGS}

    def sem(self, name):
        s = Sem()
        s.h = self.es.enter_context(self.nc.semaphore(name))
        s.n = 0
        s.name = name
        self.sems.append(s)
        return s

    def op(self, eng, meth, *args, inc=None, amt=1, **kw):
        if inc is not None:
            inc.n += amt
        h = inc.h if inc is not None else None
        site = traceback.extract_stack(limit=3)[0]
        tag = f"{meth}@{site.filename.rsplit('/',1)[-1]}:{site.lineno}"

        def f(e, meth=meth, args=args, kw=kw, h=h, amt=amt, tag=tag):
            try:
                ins = getattr(e, meth)(*args, **kw)
            except Exception as ex:
                raise RuntimeError(f"emit failed at {tag}: {ex}") from ex
            if h is not None:
                ins.then_inc(h, amt)

        self.steps[eng].append(f)
        return (inc, inc.n) if inc is not None else None

    def wait(self, eng, mark):
        if mark is None:
            return
        if isinstance(mark, list):
            for m in mark:
                self.wait(eng, m)
            return
        if isinstance(mark, Sem):
            sem, v = mark, mark.n
        else:
            sem, v = mark
        if v <= 0 or self.waited[eng].get(sem.name, 0) >= v:
            return
        self.waited[eng][sem.name] = v
        self.steps[eng].append(lambda e, h=sem.h, v=v: e.wait_ge(h, v))

    def barrier(self):
        for e in ENGS:
            for s in self.sems:
                self.wait(e, s)

    def run(self):
        with self.nc.Block() as block:
            for eng_name in ENGS:
                deco = getattr(block, eng_name)

                def body(e, steps=self.steps[eng_name]):
                    for f in steps:
                        f(e)

                deco(body)


def build_nc(cfg: CFG):
    nc = bass.Bass("TRN2", target_bir_lowering=False)
    TO, TT, NTO, NT = cfg.TO, cfg.TT, cfg.NTO, cfg.NT
    H, NI, NH, HD, HP = cfg.H, cfg.NI, cfg.NH, cfg.HD, cfg.HP
    E, C, NCT, F, FS, NFS, NFT = cfg.E, cfg.C, cfg.NCT, cfg.F, cfg.FS, cfg.NFS, cfg.NFT
    QG, NQG, NQT = cfg.QG, cfg.NQG, cfg.NQT
    SLW, NSL = cfg.SLW, cfg.NSL
    TH = TO // 2
    WW = min(512, TO)

    d_x_own = nc.dram_tensor("x_own", [TO, H], F32, kind="ExternalInput").ap()
    d_x_oth = nc.dram_tensor("x_oth", [TO, H], F32, kind="ExternalInput").ap()
    d_pos_own = nc.dram_tensor("pos_own", [128, 1], F32, kind="ExternalInput").ap()
    d_pos_oth = nc.dram_tensor("pos_oth", [128, 1], F32, kind="ExternalInput").ap()
    d_wqT = nc.dram_tensor("wqT", [128, NI * H], F32, kind="ExternalInput").ap()
    d_wkT = nc.dram_tensor("wkT", [128, HP * NI * 128], F32, kind="ExternalInput").ap()
    d_wvT = nc.dram_tensor("wvT", [128, HP * NI * 128], F32, kind="ExternalInput").ap()
    d_woT = nc.dram_tensor("woT", [128, NI * H], F32, kind="ExternalInput").ap()
    d_bq = nc.dram_tensor("bq", [128, NI], F32, kind="ExternalInput").ap()
    d_bk = nc.dram_tensor("bk", [128, NI], F32, kind="ExternalInput").ap()
    d_bv = nc.dram_tensor("bv", [1, H], F32, kind="ExternalInput").ap()
    d_bo = nc.dram_tensor("bo", [1, H], F32, kind="ExternalInput").ap()
    d_wr = nc.dram_tensor("w_router", [H, E], F32, kind="ExternalInput").ap()
    d_wi = nc.dram_tensor("wi", [E * NFS, 128, NI * FS], F32, kind="ExternalInput").ap()
    d_wo = nc.dram_tensor("wo_moe", [E * NFS, 128, NFT * H], F32, kind="ExternalInput").ap()
    d_y = nc.dram_tensor("y", [TO, H], F32, kind="ExternalOutput").ap()
    d_xnT = nc.dram_tensor("xnT", [TT // WW, 128, NI * WW], F32, kind="Internal").ap()
    d_xnh = nc.dram_tensor("xn_h", [TO, H], BF16, kind="Internal").ap()
    d_xnl = nc.dram_tensor("xn_l", [TO, H], BF16, kind="Internal").ap()
    DBG = getattr(cfg, "debug", False)
    d_ost = nc.dram_tensor("O_st", [HP, 128, TO], F32,
                           kind="ExternalOutput" if DBG else "Internal").ap()
    if DBG:
        d_dbg_probs = nc.dram_tensor("dbg_probs", [128, NTO * E], F32, kind="ExternalOutput").ap()
        d_dbg_prio = nc.dram_tensor("dbg_prio", [128, NTO * E], F32, kind="ExternalOutput").ap()
        d_dbg_gate = nc.dram_tensor("dbg_gate", [E, 128, NCT], F32, kind="ExternalOutput").ap()
        d_dbg_ei = nc.dram_tensor("dbg_ei", [128, NI * C], F32, kind="ExternalOutput").ap()
        d_dbg_eo = nc.dram_tensor("dbg_eo", [128, NCT * H], F32, kind="ExternalOutput").ap()
        d_dbg_moe = nc.dram_tensor("dbg_moe", [TO, H], F32, kind="ExternalOutput").ap()

    es = contextlib.ExitStack()
    em = Em(nc, es)

    def sb(name, shape, dt=F32, st=None):
        return (st or es).enter_context(nc.sbuf_tensor(name, shape, dt))

    PS_h = es.enter_context(nc.psum_tensor("PS", [128, 4096], F32))

    def BK(b, lo=0, n=512, p0=0, pn=128):
        return PS_h[p0:p0 + pn, 512 * b + lo: 512 * b + lo + n]

    psfree = {}

    def usebanks(*bs):
        for b in bs:
            em.wait("tensor", psfree.get(b))

    def freebanks(mark, *bs):
        for b in bs:
            psfree[b] = mark

    sv = em.sem("sv")
    ss = em.sem("ss")
    sp = em.sem("sp")
    sg = em.sem("sg")
    sW = [em.sem(f"sW{i}") for i in range(14)]
    sG = [em.sem(f"sG{i}") for i in range(10)]

    def V(meth, *a, **k):
        return em.op("vector", meth, *a, inc=sv, **k)

    def Sc(meth, *a, **k):
        return em.op("scalar", meth, *a, inc=ss, **k)

    def T(meth, *a, **k):
        return em.op("tensor", meth, *a, inc=sp, **k)

    def Tq(meth, *a, **k):
        return em.op("tensor", meth, *a, inc=None, **k)

    def G(meth, *a, **k):
        return em.op("gpsimd", meth, *a, inc=sg, **k)

    def ld(slot, out, in_):
        em.wait("sync", (sW[slot], sW[slot].n))
        return em.op("sync", "dma_start", out=out, in_=in_, inc=sW[slot], amt=16)

    def st(slot, out, in_):
        em.wait("gpsimd", (sG[slot], sG[slot].n))
        return em.op("gpsimd", "dma_start", out=out, in_=in_, inc=sG[slot], amt=16)

    def bcast(dst, srcrow, n, bank, pn=128, src_mark=None, dst_free=None):
        wt_ = em.wait
        if src_mark is not None:
            wt_("tensor", src_mark)
        mlast = None
        for o in range(0, n, 512):
            w = min(512, n - o)
            usebanks(bank)
            mm = T("matmul", BK(bank, 0, w, 0, pn), c_ones[0:1, 0:pn],
                   srcrow[0:1, o:o + w], start=True, stop=True)
            em.wait("vector", mm)
            if dst_free is not None:
                em.wait("vector", dst_free)
            mlast = V("tensor_copy", out=dst[0:pn, o:o + w], in_=BK(bank, 0, w, 0, pn))
            freebanks(mlast, bank)
        return mlast

    wv = lambda m: em.wait("vector", m)
    ws = lambda m: em.wait("scalar", m)
    wt = lambda m: em.wait("tensor", m)
    wg = lambda m: em.wait("gpsimd", m)
    wsync = lambda m: em.wait("sync", m)

    c_ident = sb("c_ident", [128, 128])
    c_iota128 = sb("c_iota128", [128, 128])
    c_rowid = sb("c_rowid", [128, 1])
    c_LT = sb("c_LT", [128, 128])
    c_ones = sb("c_ones", [128, 128])
    c_e8 = sb("c_e8", [128, E])
    c_iotaC = sb("c_iotaC", [128, C])
    c_cval = sb("c_cval", [128, NCT])
    c_ones2 = sb("c_ones2", [128, 2])
    c_iota32 = sb("c_iota32", [128, 32])
    c_invf = sb("c_invf", [128, 32])
    c_pib = sb("c_pib", [128, 1])
    c_pos = sb("c_pos", [128, 2])
    tmp_c = sb("tmp_c", [128, 4])
    tmp_i = sb("tmp_i", [128, 1], I32)

    r_probs = sb("r_probs", [128, NTO * E])
    r_probs_h = sb("r_probs_h", [128, NTO * E], BF16)
    r_probs_l = sb("r_probs_l", [128, NTO * E], BF16)
    r_prio = sb("r_prio", [128, NTO * E])
    r_qmax2 = sb("r_qmax2", [max(NQG, 2), HP])

    G("iota", c_iota128[:], pattern=[[1, 128]], base=0, channel_multiplier=0,
      allow_small_or_imprecise_dtypes=True)
    G("iota", c_rowid[:], pattern=[[0, 1]], base=0, channel_multiplier=1,
      allow_small_or_imprecise_dtypes=True)
    G("iota", c_e8[:], pattern=[[1, E]], base=0, channel_multiplier=0,
      allow_small_or_imprecise_dtypes=True)
    G("iota", c_iotaC[:], pattern=[[1, C]], base=0, channel_multiplier=0,
      allow_small_or_imprecise_dtypes=True)
    for ct in range(NCT):
        G("iota", c_cval[:, ct:ct + 1], pattern=[[0, 1]], base=ct * 128,
          channel_multiplier=1, allow_small_or_imprecise_dtypes=True)
    G("iota", c_iota32[:], pattern=[[1, 32]], base=0, channel_multiplier=0,
      allow_small_or_imprecise_dtypes=True)
    mi8 = G("iota", tmp_c[:, 0:1], pattern=[[0, 1]], base=0, channel_multiplier=1,
            allow_small_or_imprecise_dtypes=True)
    wv(mi8)
    V("memset", c_ones[:], 1.0)
    V("memset", c_pib[:], -PI)
    V("memset", r_qmax2[:], 0.0)
    V("tensor_scalar", out=c_ident[:], in0=c_iota128[:], scalar1=c_rowid[:],
      scalar2=None, op0=ALU.is_equal)
    V("tensor_scalar", out=c_LT[:], in0=c_iota128[:], scalar1=c_rowid[:],
      scalar2=None, op0=ALU.is_ge)
    V("tensor_single_scalar", c_ones2[:, 0:1], c_rowid[:], 64.0, ALU.is_lt)
    V("tensor_single_scalar", c_ones2[:, 1:2], c_rowid[:], 64.0, ALU.is_ge)
    ws(sg)
    m_invf = Sc("activation", out=c_invf[:], in_=c_iota32[:], func=AF.Exp,
                scale=-LN1E4 / 32.0)
    ld(3, c_pos[:, 0:1], d_pos_own)
    m_pos = ld(3, c_pos[:, 1:2], d_pos_oth)

    p0 = contextlib.ExitStack()
    xin = [sb(f"p0_xin{i}", [128, H], st=p0) for i in range(2)]
    xnb = [sb(f"p0_xnb{i}", [128, H], st=p0) for i in range(2)]
    WPT = WW // 128
    trsw = [sb(f"p0_trsw{i}", [128, NI * WW], st=p0) for i in range(2)]
    st6 = sb("p0_st6", [128, 12], st=p0)
    mv2 = sb("p0_mv", [128, 2], st=p0)
    lncol = sb("p0_lncol", [128, 4], st=p0)
    wr8 = sb("p0_wr8", [128, NI * E], st=p0)

    ld(3, wr8[:].rearrange("p (i e) -> p i e", i=NI),
       d_wr.rearrange("(i p) e -> p i e", p=128))
    wr_mark = (sW[3], sW[3].n)

    xh = [sb(f"p0_xh{i}", [128, H], BF16, st=p0) for i in range(2)]
    xl = [sb(f"p0_xl{i}", [128, H], BF16, st=p0) for i in range(2)]
    xin_free = [None, None]
    xnb_free = [None, None]
    trsw_free = [None, None]
    wrouter_marks = [[], []]
    xh_free = [None, None]
    xl_free = [None, None]
    for tt in range(NT):
        own = tt < NTO
        b = tt % 2
        src = d_x_own if own else d_x_oth
        row0 = (tt % NTO) * 128
        wsync(xin_free[b])
        ld(b, xin[b][:], src[row0:row0 + 128, :])
        wv((sW[b], sW[b].n))
        V("bn_stats", st6[:, 0:6], xin[b][:, 0:512])
        V("bn_stats", st6[:, 6:12], xin[b][:, 512:1024])
        wv(sv)
        V("bn_aggr", mv2[:], st6[:])
        wv(sv)
        V("tensor_single_scalar", lncol[:, 0:1], mv2[:, 1:2], 1e-5, ALU.add)
        ws(sv)
        Sc("activation", out=lncol[:, 1:2], in_=lncol[:, 0:1], func=AF.Sqrt)
        wv(ss)
        V("reciprocal", lncol[:, 2:3], lncol[:, 1:2])
        wv(sv)
        wv(xnb_free[b])
        mxn = V("tensor_scalar", out=xnb[b][:], in0=xin[b][:], scalar1=mv2[:, 0:1],
                scalar2=lncol[:, 2:3], op0=ALU.subtract, op1=ALU.mult)
        xin_free[b] = mxn
        mst = None
        if own:
            wv(sv)
            wv(xh_free[b])
            V("tensor_copy", out=xh[b][:], in_=xnb[b][:])
            wv(sv)
            mlo = V("tensor_tensor", out=xl[b][:], in0=xnb[b][:], in1=xh[b][:],
                    op=ALU.subtract)
            wg(mlo)
            msth = st(b, d_xnh[row0:row0 + 128, :], xh[b][:])
            mst = st(2 + b, d_xnl[row0:row0 + 128, :], xl[b][:])
            xh_free[b] = [msth, mst]
        wt(mxn)
        usebanks(2 * b, 2 * b + 1)
        for j in range(NI):
            mtr = T("transpose", PS_h[:, b * 1024 + j * 128: b * 1024 + j * 128 + 128],
                    xnb[b][:, j * 128:j * 128 + 128], c_ident[:])
        w_ = tt // WPT
        wb = w_ % 2
        ti = tt % WPT
        if ti == 0:
            wrouter_marks[wb] = []
        wv(mtr)
        wv(trsw_free[wb] if ti == 0 else None)
        mev = V("tensor_copy",
                out=trsw[wb][:].rearrange("p (j t) -> p j t", j=NI)[:, :, ti * 128:(ti + 1) * 128],
                in_=PS_h[:, b * 1024: b * 1024 + H].rearrange("p (j t) -> p j t", j=NI))
        freebanks(mev, 2 * b, 2 * b + 1)
        xnb_free[b] = [mev] + ([mst] if mst else [])
        if ti == WPT - 1:
            wg(mev)
            mw = st(4 + wb, d_xnT[w_], trsw[wb][:])
            trsw_free[wb] = [mw] + wrouter_marks[wb]
        if own:
            wt(wr_mark)
            wt(mev)
            usebanks(4)
            for j in range(NI):
                mm = (T if j == NI - 1 else Tq)(
                    "matmul", BK(4, 0, E),
                    trsw[wb][:, j * WW + ti * 128: j * WW + ti * 128 + 128],
                    wr8[:, j * E:(j + 1) * E], start=(j == 0), stop=(j == NI - 1))
            wrouter_marks[wb].append(mm)
            if ti == WPT - 1:
                trsw_free[wb].append(mm)
            wv(mm)
            V("reduce_max", tmp_c[:, 0:1], BK(4, 0, E), axis=mybir.AxisListType.X)
            wv(sv)
            V("tensor_single_scalar", tmp_c[:, 1:2], tmp_c[:, 0:1], -1.0, ALU.mult)
            ws(sv)
            mex = Sc("activation", out=r_probs[:, tt * E:(tt + 1) * E], in_=BK(4, 0, E),
                     func=AF.Exp, bias=tmp_c[:, 1:2], accum_out=tmp_c[:, 2:3])
            freebanks(mex, 4)
            wv(mex)
            V("reciprocal", tmp_c[:, 3:4], tmp_c[:, 2:3])
            wv(sv)
            V("tensor_scalar", out=r_probs[:, tt * E:(tt + 1) * E],
              in0=r_probs[:, tt * E:(tt + 1) * E], scalar1=tmp_c[:, 3:4],
              scalar2=None, op0=ALU.mult)

    em.barrier()
    p0.close()

    pr = contextlib.ExitStack()
    top8 = sb("pr_top8", [128, 8], st=pr)
    idxu = sb("pr_idxu", [128, 8], U32, st=pr)
    idxf = sb("pr_idxf", [128, 8], st=pr)
    idx12 = sb("pr_idx12", [128, 2 * NTO], st=pr)
    maskkt = sb("pr_mask", [128, NT * E], st=pr)
    priokt = sb("pr_priokt", [128, NT * E], st=pr)

    for tt in range(NTO):
        pv = r_probs[:, tt * E:(tt + 1) * E]
        wv(sv)
        V("max", out=top8[:], in_=pv)
        wv(sv)
        V("max_index", out=idxu[:], in_max=top8[:], in_values=pv)
        wv(sv)
        V("tensor_copy", out=idxf[:], in_=idxu[:])
        wv(sv)
        V("tensor_copy", out=idx12[:, tt:tt + 1], in_=idxf[:, 0:1])
        V("tensor_copy", out=idx12[:, NTO + tt:NTO + tt + 1], in_=idxf[:, 1:2])
    wv(sv)
    for j in range(NT):
        V("tensor_scalar", out=maskkt[:, j * E:(j + 1) * E], in0=c_e8[:],
          scalar1=idx12[:, j:j + 1], scalar2=None, op0=ALU.is_equal)
    mmask = (sv, sv.n)
    wt(mmask)
    for j in range(NT):
        usebanks(0)
        for i in range(j):
            Tq("matmul", BK(0, 0, E), c_ones[:], maskkt[:, i * E:(i + 1) * E],
               start=(i == 0), stop=False)
        mcj = T("matmul", BK(0, 0, E), c_LT[:], maskkt[:, j * E:(j + 1) * E],
                start=(j == 0), stop=True)
        wv(mcj)
        V("tensor_tensor", out=priokt[:, j * E:(j + 1) * E], in0=BK(0, 0, E),
          in1=maskkt[:, j * E:(j + 1) * E], op=ALU.mult)
        mpr_ = (sv, sv.n)
        freebanks(mpr_, 0)
        wv(sv)
        V("tensor_single_scalar", priokt[:, j * E:(j + 1) * E],
          priokt[:, j * E:(j + 1) * E], 1.0, ALU.subtract)
    wv(sv)
    for tt in range(NTO):
        V("tensor_max", out=r_prio[:, tt * E:(tt + 1) * E],
          in0=priokt[:, tt * E:(tt + 1) * E],
          in1=priokt[:, (NTO + tt) * E:(NTO + tt + 1) * E])

    wv(sv)
    V("tensor_copy", out=r_probs_h[:], in_=r_probs[:])
    wv(sv)
    V("tensor_tensor", out=r_probs_l[:], in0=r_probs[:], in1=r_probs_h[:],
      op=ALU.subtract)
    if DBG:
        wg(sv)
        st(3, d_dbg_probs, r_probs[:])
        st(3, d_dbg_prio, r_prio[:])
    em.barrier()
    pr.close()

    pa = contextlib.ExitStack()
    cos_feat = sb("pa_cosf", [128, TT], st=pa)
    ssn_feat = sb("pa_ssnf", [128, TT], st=pa)
    tcol = sb("pa_tcol", [128, 5], st=pa)
    Q_sb = sb("pa_Qh", [128, HP * TO], BF16, st=pa)
    Q_lo = sb("pa_Ql", [128, HP * TO], BF16, st=pa)
    ptab = contextlib.ExitStack()
    iota_t = sb("pa_iott", [128, TO], st=ptab)
    rbig = sb("pa_rbig", [128, TO], st=ptab)
    mskb = sb("pa_mskb", [128, TO], st=ptab)

    wv(m_pos)
    wv(m_invf)
    G("iota", iota_t[:], pattern=[[1, TO]], base=0, channel_multiplier=0,
      allow_small_or_imprecise_dtypes=True)
    mgt = G("iota", tcol[:, 0:1], pattern=[[0, 1]], base=0, channel_multiplier=1,
            allow_small_or_imprecise_dtypes=True)
    wv(mgt)
    V("tensor_single_scalar", tcol[:, 1:2], tcol[:, 0:1], 64.0, ALU.is_ge)
    wv(sv)
    V("scalar_tensor_tensor", out=tcol[:, 1:2], in0=tcol[:, 1:2], scalar=-64.0,
      in1=tcol[:, 0:1], op0=ALU.mult, op1=ALU.add)
    wv(sv)
    V("tensor_single_scalar", tcol[:, 2:3], tcol[:, 1:2], 0.5, ALU.mult)
    wv(sv)
    V("tensor_copy", out=tcol[:, 3:4], in_=tcol[:, 2:3])
    for s_ in (32.0, 16.0, 8.0, 4.0, 2.0, 1.0):
        wv(sv)
        V("tensor_single_scalar", tcol[:, 4:5], tcol[:, 3:4], s_, ALU.is_ge)
        wv(sv)
        V("scalar_tensor_tensor", out=tcol[:, 3:4], in0=tcol[:, 4:5], scalar=-s_,
          in1=tcol[:, 3:4], op0=ALU.mult, op1=ALU.add)
    wv(sv)
    V("tensor_tensor", out=tcol[:, 2:3], in0=tcol[:, 2:3], in1=tcol[:, 3:4],
      op=ALU.subtract)
    wv(sv)
    V("tensor_scalar", out=tcol[:, 3:4], in0=tcol[:, 3:4], scalar1=4.0,
      scalar2=-1.0, op0=ALU.mult, op1=ALU.add)
    ws(sv)
    m_ic = Sc("activation", out=tcol[:, 1:2], in_=tcol[:, 2:3], func=AF.Exp,
              scale=-LN1E4 / 32.0)
    nred = max(1, int(np.ceil(np.log2(max(2.0, (TT + 64) / TWO_PI)))))
    for half in range(2):
        pcol = c_pos[:, 0:1] if half == 0 else c_pos[:, 1:2]
        c0 = half * TO
        wv(m_ic)
        wv(sv)
        V("tensor_scalar", out=rbig[:], in0=iota_t[:], scalar1=pcol,
          scalar2=tcol[:, 1:2], op0=ALU.add, op1=ALU.mult)
        for kstep in range(nred - 1, -1, -1):
            s_ = TWO_PI * (2.0 ** kstep)
            wv(sv)
            V("tensor_single_scalar", mskb[:], rbig[:], s_, ALU.is_ge)
            wv(sv)
            V("scalar_tensor_tensor", out=rbig[:], in0=mskb[:], scalar=-s_,
              in1=rbig[:], op0=ALU.mult, op1=ALU.add)
        wv(sv)
        V("tensor_scalar", out=rbig[:], in0=rbig[:], scalar1=0.0,
          scalar2=TWO_PI - 1e-5, op0=ALU.max, op1=ALU.min)
        ws(sv)
        msin = Sc("activation", out=ssn_feat[:, c0:c0 + TO], in_=rbig[:],
                  func=AF.Sin, bias=c_pib[:], scale=1.0)
        wv(msin)
        V("tensor_single_scalar", rbig[:], rbig[:], PI / 2, ALU.add)
        wv(sv)
        V("tensor_single_scalar", mskb[:], rbig[:], TWO_PI, ALU.is_ge)
        wv(sv)
        V("scalar_tensor_tensor", out=rbig[:], in0=mskb[:], scalar=-TWO_PI,
          in1=rbig[:], op0=ALU.mult, op1=ALU.add)
        wv(sv)
        V("tensor_scalar", out=rbig[:], in0=rbig[:], scalar1=0.0,
          scalar2=TWO_PI - 1e-5, op0=ALU.max, op1=ALU.min)
        ws(sv)
        Sc("activation", out=cos_feat[:, c0:c0 + TO], in_=rbig[:],
           func=AF.Sin, bias=c_pib[:], scale=1.0)
        wv(ss)
        V("tensor_scalar", out=ssn_feat[:, c0:c0 + TO], in0=ssn_feat[:, c0:c0 + TO],
          scalar1=tcol[:, 3:4], scalar2=-1.0, op0=ALU.mult, op1=ALU.mult)
        V("tensor_single_scalar", cos_feat[:, c0:c0 + TO], cos_feat[:, c0:c0 + TO],
          -1.0, ALU.mult)
    mtab = (sv, sv.n)
    em.barrier()
    ptab.close()
    half_off = 0
    qshf = None

    xnw = [sb(f"pa_xnw{i}", [128, NI * 512], st=pa) for i in range(2)]
    qtmp = [sb(f"pa_qtmp{i}", [128, 512], st=pa) for i in range(2)]
    qsh_b = sb("pa_qsh", [128, 512], st=pa)
    qcs_b = sb("pa_qcs", [128, 512], st=pa)
    sq512 = qcs_b
    bqc8 = sb("pa_bqc8", [128, NI], st=pa)
    pa0 = contextlib.ExitStack()
    wqT_sb = sb("pa_wqT", [128, NI * H], st=pa0)

    mwq = []
    for sl_, q_ in zip((2, 8, 10, 12), range(4)):
        ld(sl_, wqT_sb[:, q_ * (NI * H // 4):(q_ + 1) * (NI * H // 4)],
           d_wqT[:, q_ * (NI * H // 4):(q_ + 1) * (NI * H // 4)])
        mwq.append((sW[sl_], sW[sl_].n))
    ld(3, bqc8[:], d_bq)
    mbq = (sW[3], sW[3].n)
    SWAP = [i ^ 1 for i in range(32)]

    xnw_free = [None, None]
    qn2 = sb("pa_qn2", [2, TO], st=pa0)
    qn2w = sb("pa_qn2w", [2, max(NQG, 2)], st=pa0)
    qn2t = sb("pa_qn2t", [max(NQG, 2), 2], st=pa0)

    qp_free = [None, None]
    for w in range(TO // WW):
        bw = w % 2
        wsync(xnw_free[bw])
        ld(bw, xnw[bw][:, 0:NI * WW], d_xnT[w])
        mx = (sW[bw], sW[bw].n)
        wt(mx)
        wt(mwq)
        lastq = None
        for ho in range(NI):
            bq_ = ho % 2
            usebanks(bq_)
            for j in range(NI):
                mm = (T if j == NI - 1 else Tq)(
                    "matmul", BK(bq_, 0, WW),
                    wqT_sb[:, j * H + ho * 128: j * H + ho * 128 + 128],
                    xnw[bw][:, j * WW:(j + 1) * WW],
                    start=(j == 0), stop=(j == NI - 1))
            lastq = mm
            ws(mm)
            ws(mbq)
            ws(qp_free[bq_])
            mqe = Sc("activation", out=qtmp[bq_][:, 0:WW], in_=BK(bq_, 0, WW),
                     func=AF.Identity, bias=bqc8[:, ho:ho + 1], scale=1.0)
            freebanks(mqe, bq_)
            csl = cos_feat[:, half_off + w * WW: half_off + (w + 1) * WW]
            ssl = ssn_feat[:, half_off + w * WW: half_off + (w + 1) * WW]
            wv(mqe)
            wv(qshf)
            V("stream_shuffle", out=qsh_b[:, 0:WW], in_=qtmp[bq_][:, 0:WW],
              mask=SWAP)
            V("tensor_tensor", out=qcs_b[:, 0:WW], in0=qtmp[bq_][:, 0:WW],
              in1=csl, op=ALU.mult)
            wv(sv)
            V("tensor_tensor", out=qsh_b[:, 0:WW], in0=qsh_b[:, 0:WW],
              in1=ssl, op=ALU.mult)
            wv(sv)
            V("tensor_tensor", out=qtmp[bq_][:, 0:WW], in0=qcs_b[:, 0:WW],
              in1=qsh_b[:, 0:WW], op=ALU.add)
            wv(sv)
            V("tensor_copy", out=Q_sb[:, ho * TO + w * WW: ho * TO + (w + 1) * WW],
              in_=qtmp[bq_][:, 0:WW])
            wv(sv)
            mql = V("scalar_tensor_tensor",
                    out=Q_lo[:, ho * TO + w * WW: ho * TO + (w + 1) * WW],
                    in0=Q_sb[:, ho * TO + w * WW: ho * TO + (w + 1) * WW],
                    scalar=-1.0, in1=qtmp[bq_][:, 0:WW], op0=ALU.mult, op1=ALU.add)
            qp_free[bq_] = mql
            qshf = mql
        xnw_free[bw] = [lastq]
    mQ = (sv, sv.n)

    for hp in range(HP):
        for sl in range(NSL):
            wv(mQ)
            wv(sv)
            V("tensor_tensor", out=sq512[:, 0:SLW],
              in0=Q_sb[:, hp * TO + sl * SLW: hp * TO + (sl + 1) * SLW],
              in1=Q_sb[:, hp * TO + sl * SLW: hp * TO + (sl + 1) * SLW], op=ALU.mult)
            msq = (sv, sv.n)
            wt(msq)
            usebanks(0)
            mm = T("matmul", BK(0, 0, SLW, 0, 2), c_ones2[:], sq512[:, 0:SLW],
                   start=True, stop=True)
            wv(mm)
            mev = V("tensor_copy", out=qn2[:, sl * SLW:(sl + 1) * SLW],
                    in_=BK(0, 0, SLW, 0, 2))
            freebanks(mev, 0)
        wv(sv)
        for qg in range(NQG):
            V("reduce_max", qn2w[:, qg:qg + 1], qn2[:, qg * QG:(qg + 1) * QG],
              axis=mybir.AxisListType.X)
        mqw = (sv, sv.n)
        wt(mqw)
        usebanks(1)
        mtr = T("transpose", BK(1, 0, 2, 0, NQG), qn2w[0:2, 0:NQG], c_ident[0:2, 0:2])
        wv(mtr)
        V("tensor_copy", out=qn2t[0:NQG, :], in_=BK(1, 0, 2, 0, NQG))
        mn = (sv, sv.n)
        freebanks(mn, 1)
        wv(sv)
        V("tensor_max", out=r_qmax2[0:NQG, hp:hp + 1], in0=qn2t[0:NQG, 0:1],
          in1=qn2t[0:NQG, 1:2])

    em.barrier()
    pa0.close()

    K_sb = sb("pa_Kh", [128, TT], BF16, st=pa)
    K_lo = sb("pa_Kl", [128, TT], BF16, st=pa)
    V_sb = sb("pa_V", [128, NT * 130], st=pa)
    P_sb = [sb(f"pa_P{i}", [128, 2 * QG], st=pa) for i in range(3)]
    wk_sb = sb("pa_wk", [128, NI * 128], st=pa)
    wv_sb2 = sb("pa_wv", [128, NI * 128], st=pa)
    kn2c = sb("pa_kn2c", [2, 4], st=pa)
    nb4 = sb("pa_nb4", [max(NQG, 2), 4], st=pa)
    nbcol = sb("pa_nbcol", [128, NQG], st=pa)
    nbT = sb("pa_nbT", [1, max(NQG, 2)], st=pa)
    ost_st = sb("pa_ost", [128, 2 * QG], st=pa)
    bcl_sb = sb("pa_bcl", [128, QG], st=pa)
    bkc8 = sb("pa_bkc8", [128, HP], st=pa)
    bvrow = sb("pa_bvrow", [1, 128], st=pa)
    bvc = sb("pa_bvc", [128, 128], st=pa)

    ld(4, bkc8[:], d_bk)
    mbk8 = (sW[4], sW[4].n)
    V("memset", V_sb[:].rearrange("p (t u) -> p t u", u=130)[:, :, 64:65], 1.0)
    mones = V("memset", V_sb[:].rearrange("p (t u) -> p t u", u=130)[:, :, 129:130], 1.0)

    def ld_wkv(hp_):
        ld(2, wk_sb[:], d_wkT[:, hp_ * NI * 128:(hp_ + 1) * NI * 128])
        ld(3, wv_sb2[:], d_wvT[:, hp_ * NI * 128:(hp_ + 1) * NI * 128])
        return [(sW[2], sW[2].n), (sW[3], sW[3].n)]

    P_free = [None, None, None]
    O_free = None
    pending_O = [None, None]
    hp_of_qg = {}
    prod_free = None
    kp_free = [None, None]
    mwkv_n = ld_wkv(0)
    for hp in range(HP):
        if pending_O[0] is not None:
            O_free = evac_O(pending_O[0], pending_O[1])
            pending_O[0] = None
        attn_prev = (sp, sp.n)
        sc_prev = (ss, ss.n)
        hb = hp % 2
        mwkv = mwkv_n
        wsync(prod_free)
        wsync(mbv if hp else None)
        ld(5, bvrow[:], d_bv[:, hp * 128:(hp + 1) * 128])
        mbr = (sW[5], sW[5].n)
        wt(mbr)
        mbv = bcast(bvc, bvrow[:], 128, 5, dst_free=prod_free)
        kn_mark = None
        NWV = WW // 128
        for w in range(TT // WW):
            bw = w % 2
            wsync(xnw_free[bw])
            ld(bw, xnw[bw][:, 0:NI * WW], d_xnT[w])
            mx = (sW[bw], sW[bw].n)
            wt(mx)
            wt(mwkv)
            bk_ = 4 + bw
            usebanks(bk_)
            for j in range(NI):
                mmk = (T if j == NI - 1 else Tq)(
                    "matmul", BK(bk_, 0, WW),
                    wk_sb[:, j * 128:(j + 1) * 128],
                    xnw[bw][:, j * WW:(j + 1) * WW],
                    start=(j == 0), stop=(j == NI - 1))
            ws(mmk)
            ws(mbk8)
            ws(kp_free[bw])
            mke = Sc("activation", out=qtmp[bw][:, 0:WW], in_=BK(bk_, 0, WW),
                     func=AF.Identity, bias=bkc8[:, hp:hp + 1], scale=1.0)
            freebanks(mke, bk_)
            csl = cos_feat[:, w * WW:(w + 1) * WW]
            ssl = ssn_feat[:, w * WW:(w + 1) * WW]
            wv(attn_prev)
            wv(mke)
            wv(qshf)
            V("stream_shuffle", out=qsh_b[:, 0:WW], in_=qtmp[bw][:, 0:WW],
              mask=SWAP)
            V("tensor_tensor", out=qcs_b[:, 0:WW], in0=qtmp[bw][:, 0:WW],
              in1=csl, op=ALU.mult)
            wv(sv)
            V("tensor_tensor", out=qsh_b[:, 0:WW], in0=qsh_b[:, 0:WW],
              in1=ssl, op=ALU.mult)
            wv(sv)
            V("tensor_tensor", out=qtmp[bw][:, 0:WW], in0=qcs_b[:, 0:WW],
              in1=qsh_b[:, 0:WW], op=ALU.add)
            wv(sv)
            V("tensor_copy", out=K_sb[:, w * WW:(w + 1) * WW],
              in_=qtmp[bw][:, 0:WW])
            wv(sv)
            mkl = V("scalar_tensor_tensor", out=K_lo[:, w * WW:(w + 1) * WW],
                    in0=K_sb[:, w * WW:(w + 1) * WW], scalar=-1.0,
                    in1=qtmp[bw][:, 0:WW], op0=ALU.mult, op1=ALU.add)
            kp_free[bw] = mkl
            qshf = mkl
            for tv in range(NWV):
                bv_ = 6 + (tv % 2)
                usebanks(bv_)
                for j in range(NI):
                    mmv = (T if j == NI - 1 else Tq)(
                        "matmul", BK(bv_, 0, 128),
                        xnw[bw][:, j * WW + tv * 128: j * WW + tv * 128 + 128],
                        wv_sb2[:, j * 128:(j + 1) * 128],
                        start=(j == 0), stop=(j == NI - 1))
                tt_ = w * NWV + tv
                wv(mmv)
                wv(mbv)
                wv(mones)
                V("tensor_tensor", out=V_sb[:, tt_ * 130: tt_ * 130 + 64],
                  in0=BK(bv_, 0, 64), in1=bvc[:, 0:64], op=ALU.add)
                mvev = V("tensor_tensor", out=V_sb[:, tt_ * 130 + 65: tt_ * 130 + 129],
                         in0=BK(bv_, 64, 64), in1=bvc[:, 64:128], op=ALU.add)
                freebanks(mvev, bv_)
            xnw_free[bw] = [mmv]
            wv(sv)
            V("tensor_tensor", out=sq512[:, 0:WW], in0=K_sb[:, w * WW:(w + 1) * WW],
              in1=K_sb[:, w * WW:(w + 1) * WW], op=ALU.mult)
            msq = (sv, sv.n)
            wt(msq)
            usebanks(7)
            mm = T("matmul", BK(7, 0, WW, 0, 2), c_ones2[:], sq512[:, 0:WW],
                   start=True, stop=True)
            wv(mm)
            V("reduce_max", kn2c[:, 1:2], BK(7, 0, WW, 0, 2),
              axis=mybir.AxisListType.X)
            mrm = (sv, sv.n)
            freebanks(mrm, 7)
            wv(sv)
            if w == 0:
                kn_mark = V("tensor_copy", out=kn2c[:, 0:1], in_=kn2c[:, 1:2])
            else:
                kn_mark = V("tensor_max", out=kn2c[:, 0:1], in0=kn2c[:, 0:1],
                            in1=kn2c[:, 1:2])
                prod_free = kn_mark
        if hp + 1 < HP:
            wsync((sp, sp.n))
            mwkv_n = ld_wkv(hp + 1)
        wt(kn_mark)
        usebanks(4)
        mtr = T("transpose", BK(4, 0, 2, 0, 1), kn2c[0:2, 0:1], c_ident[0:2, 0:2])
        wv(mtr)
        V("reduce_max", kn2c[0:1, 2:3], BK(4, 0, 2, 0, 1), axis=mybir.AxisListType.X)
        mkm = (sv, sv.n)
        freebanks(mkm, 4)
        wv(mkm)
        mkb4 = bcast(nb4[:, 1:2], kn2c[0:1, 2:3], 1, 5, pn=max(NQG, 2),
                     src_mark=mkm)
        wv(mkb4)
        V("tensor_tensor", out=nb4[:, 0:1], in0=r_qmax2[:, hp:hp + 1],
          in1=nb4[:, 1:2], op=ALU.mult)
        ws(sv)
        Sc("activation", out=nb4[:, 2:3], in_=nb4[:, 0:1], func=AF.Sqrt)
        wv(ss)
        mnb = V("tensor_single_scalar", nb4[:, 3:4], nb4[:, 2:3], -0.125, ALU.mult)
        wt(mnb)
        usebanks(4)
        mtr2 = T("transpose", BK(4, 0, max(NQG, 2), 0, 1), nb4[:, 3:4],
                 c_ident[0:max(NQG, 2), 0:max(NQG, 2)])
        wv(mtr2)
        mnbt = V("tensor_copy", out=nbT[0:1, :], in_=BK(4, 0, max(NQG, 2), 0, 1))
        freebanks(mnbt, 4)
        for qg in range(NQG):
            mbc = bcast(nbcol[:, qg:qg + 1], nbT[:, qg:qg + 1], 1, 5,
                        src_mark=mnbt, dst_free=sc_prev)
        def evac_O(qg, mo_):
            wv(mo_)
            wv((sG[0], sG[0].n))
            for h in range(2):
                wv(sv)
                V("reciprocal", bcl_sb[64:65, 0:QG], BK(4 + h, 0, QG, 64, 1))
                wv(sv)
                mrc = (sv, sv.n)
                wt(mrc)
                usebanks(6)
                mbl = T("matmul", BK(6, 0, QG, 0, 64), c_ones[64:65, 0:64],
                        bcl_sb[64:65, 0:QG], start=True, stop=True)
                wv(mbl)
                V("tensor_copy", out=bcl_sb[0:64, 0:QG], in_=BK(6, 0, QG, 0, 64))
                mcl = (sv, sv.n)
                freebanks(mcl, 6)
                wv(sv)
                mev = V("tensor_tensor", out=ost_st[0:64, h * QG:(h + 1) * QG],
                        in0=BK(4 + h, 0, QG, 0, 64), in1=bcl_sb[0:64, 0:QG],
                        op=ALU.mult)
            freebanks(mev, 4, 5)
            wg(mev)
            for h in range(2):
                st(0, d_ost[hp_of_qg[qg], h * 64:(h + 1) * 64, qg * QG:(qg + 1) * QG],
                   ost_st[0:64, h * QG:(h + 1) * QG])
            return mev

        for qg in range(NQG):
            mex = {}
            for kt in range(NT + 2):
                if kt < NT:
                    sbuf_i = kt % 3
                    b0 = 2 * (kt % 2)
                    usebanks(b0, b0 + 1)
                    qsl = slice(hp * TO + qg * QG, hp * TO + (qg + 1) * QG)
                    ksl = slice(kt * 128, (kt + 1) * 128)
                    onebank = (2 * QG <= 512)
                    for pi_, (kk_, qq_) in enumerate(
                            ((K_sb, Q_sb), (K_sb, Q_lo), (K_lo, Q_sb))):
                        ma = (T if pi_ == 2 else Tq)(
                            "matmul", PS_h[:, b0 * 512: b0 * 512 + QG],
                            kk_[0:64, ksl], qq_[0:64, qsl],
                            start=(pi_ == 0),
                            stop=(pi_ == 2 and not onebank), tile_position=(0, 0))
                        mb = (T if pi_ == 2 else Tq)(
                            "matmul", PS_h[:, b0 * 512 + QG: b0 * 512 + 2 * QG],
                            kk_[64:128, ksl], qq_[64:128, qsl],
                            start=(pi_ == 0 and not onebank), stop=(pi_ == 2),
                            tile_position=(64, 0))
                    ws([ma, mb, mbc])
                    ws(P_free[sbuf_i])
                    mex[kt] = Sc("activation", out=P_sb[sbuf_i][:],
                                 in_=PS_h[:, b0 * 512: b0 * 512 + 2 * QG], func=AF.Exp,
                                 scale=0.125, bias=nbcol[:, qg:qg + 1])
                    freebanks(mex[kt], b0, b0 + 1)
                if kt == 1 and pending_O[0] is not None:
                    O_free = evac_O(pending_O[0], pending_O[1])
                    pending_O[0] = None
                if kt >= 2:
                    kp = kt - 2
                    pi = kp % 3
                    wt(mex[kp])
                    if kp == 0:
                        wt(O_free)
                        usebanks(4, 5)
                    for h in range(2):
                        mo = (T if (h == 1 or kp == NT - 1) else Tq)(
                            "matmul", BK(4 + h, 0, QG, 0, 65),
                            V_sb[:, kp * 130 + h * 65: kp * 130 + (h + 1) * 65],
                            P_sb[pi][:, h * QG: h * QG + QG],
                            start=(kp == 0), stop=(kp == NT - 1))
                    P_free[pi] = mo
            hp_of_qg[qg] = hp
            pending_O[0] = qg
            pending_O[1] = mo

    if pending_O[0] is not None:
        O_free = evac_O(pending_O[0], pending_O[1])
        pending_O[0] = None
    em.barrier()
    pa.close()

    pm = contextlib.ExitStack()
    out_acc = sb("pm_out", [128, NTO * H], st=pm)
    pmm = contextlib.ExitStack()
    xnstr = [[sb(f"pm_xn{i}{p}", [128, H], BF16, st=pmm) for i in range(3)]
             for p in range(2)]
    dispb = [sb(f"pm_disp{i}", [128, C], BF16, st=pmm) for i in range(2)]
    ei_h = sb("pm_eih", [128, NI * C], BF16, st=pmm)
    ei_l = sb("pm_eil", [128, NI * C], BF16, st=pmm)
    hact_f = sb("pm_hactf", [128, NFT * C], st=pmm)
    hact_h = [sb(f"pm_hacth{i}", [128, NFT * C], BF16, st=pmm) for i in range(2)]
    hact_l = [sb(f"pm_hactl{i}", [128, NFT * C], BF16, st=pmm) for i in range(2)]
    eo_sb = sb("pm_eo", [128, NCT * H], st=pmm)
    eo_h = sb("pm_eoh", [128, NCT * H], BF16, st=pmm)
    eo_l = sb("pm_eol", [128, NCT * H], BF16, st=pmm)
    gate_sb = sb("pm_gate", [128, NCT], st=pmm)
    wi_h = [sb(f"pm_wih{i}", [128, NI * FS], BF16, st=pmm) for i in range(2)]
    wi_l = [sb(f"pm_wil{i}", [128, NI * FS], BF16, st=pmm) for i in range(2)]
    wo_h = [sb(f"pm_woh{i}", [128, NFT * H], BF16, st=pmm) for i in range(2)]
    wo_l = [sb(f"pm_wol{i}", [128, NFT * H], BF16, st=pmm) for i in range(2)]
    wfi = [sb(f"pm_wfi{i}", [128, NI * FS], st=pmm) for i in range(2)]
    dispTg = [sb(f"pm_dispTg{i}", [128, NCT * 128], BF16, st=pmm) for i in range(2)]
    c_ident_bf = sb("pm_identbf", [128, 128], BF16, st=pmm)
    midbf = V("tensor_copy", out=c_ident_bf[:], in_=c_ident[:])

    out_mark = V("memset", out_acc[:], 0.0)

    xn_free = [None, None, None]
    disp_free = [None, None]
    wi_free = [None, None]
    wo_free = [None, None]
    wfi_free = [None, None]
    wih_free = [None, None]
    wil_free = [None, None]
    woh_free = [None, None]
    wol_free = [None, None]
    hact_rd = [None, None]
    hact_f_rd = [None, None]
    dTg_rd = [None, None]
    ei_free = None
    gate_free = None
    for e in range(E):
        usebanks(0)
        for tt in range(NTO):
            b = tt % 2
            wv(disp_free[b])
            wv(sv)
            mdg = V("tensor_scalar", out=dispb[b][:], in0=c_iotaC[:],
                    scalar1=r_prio[:, tt * E + e: tt * E + e + 1], scalar2=None,
                    op0=ALU.is_equal)
            wt(mdg)
            for ct in range(NCT):
                for ph, pr_ in ((0, r_probs_h), (1, r_probs_l)):
                    mgm = (T if (ct == NCT - 1 or tt == NTO - 1) else Tq)(
                        "matmul", BK(0, ct, 1), dispb[b][:, ct * 128:(ct + 1) * 128],
                        pr_[:, tt * E + e: tt * E + e + 1],
                        start=(tt == 0 and ct == 0 and ph == 0),
                        stop=(tt == NTO - 1 and ct == NCT - 1 and ph == 1))
            disp_free[b] = mgm
        wv(mgm)
        mgev = V("tensor_copy", out=gate_sb[:], in_=BK(0, 0, NCT))
        gate_free = mgev
        freebanks(mgev, 0)
        if DBG:
            wg(mgev)
            st(3, d_dbg_gate[e], gate_sb[:])
        usebanks(*range(NI))

        def ld_xn(tt):
            b3 = tt % 3
            wsync(xn_free[b3])
            ld(b3 % 2, xnstr[0][b3][:], d_xnh[tt * 128:(tt + 1) * 128, :])
            ld(6 + b3 % 2, xnstr[1][b3][:], d_xnl[tt * 128:(tt + 1) * 128, :])
            return [(sW[b3 % 2], sW[b3 % 2].n), (sW[6 + b3 % 2], sW[6 + b3 % 2].n)]

        mxs = {0: ld_xn(0), 1: ld_xn(1)}
        for tt in range(NTO):
            b = tt % 2
            b3 = tt % 3
            if tt + 2 < NTO:
                mxs[tt + 2] = ld_xn(tt + 2)
            wv(disp_free[b])
            wv(sv)
            mdg = V("tensor_scalar", out=dispb[b][:], in0=c_iotaC[:],
                    scalar1=r_prio[:, tt * E + e: tt * E + e + 1], scalar2=None,
                    op0=ALU.is_equal)
            wt(mdg)
            wt(mxs[tt])
            for ph in range(2):
                for it in range(NI):
                    mdm = (T if (it == NI - 1 or tt == NTO - 1) else Tq)(
                        "matmul", BK(it, 0, C), xnstr[ph][b3][:, it * 128:(it + 1) * 128],
                        dispb[b][:], start=(tt == 0 and ph == 0),
                        stop=(tt == NTO - 1 and ph == 1))
            xn_free[b3] = mdm
            disp_free[b] = mdm
        wv(mdm)
        for it in range(NI):
            V("tensor_copy", out=ei_h[:, it * C:(it + 1) * C], in_=BK(it, 0, C))
            wv(sv)
            mei = V("scalar_tensor_tensor", out=ei_l[:, it * C:(it + 1) * C],
                    in0=ei_h[:, it * C:(it + 1) * C], scalar=-1.0,
                    in1=BK(it, 0, C), op0=ALU.mult, op1=ALU.add)
        ei_free = mei
        freebanks(mei, *range(NI))
        if DBG and e == 0:
            wg(mei)
            st(3, d_dbg_ei, ei_sb[:])
        def ld_strip(s):
            b = s % 2
            W4 = NI * FS // 4
            W4o = NFT * H // 4
            wsync(wi_free[b])
            wsync(wfi_free[b])
            mldi = []
            for sl_, q_ in zip((2, 8, 10, 12), range(4)):
                ld(sl_, wfi[b][:, q_ * W4:(q_ + 1) * W4],
                   d_wi[e * NFS + s][:, q_ * W4:(q_ + 1) * W4])
                mldi.append((sW[sl_], sW[sl_].n))
            wv(mldi)
            wv(wi_free[b])
            V("tensor_copy", out=wi_h[b][:], in_=wfi[b][:])
            wv(sv)
            mspi = V("scalar_tensor_tensor", out=wi_l[b][:], in0=wi_h[b][:],
                     scalar=-1.0, in1=wfi[b][:], op0=ALU.mult, op1=ALU.add)
            wsync(wo_free[b])
            wsync(mspi)
            mldo = []
            for sl_, q_ in zip((3, 9, 11, 13), range(4)):
                ld(sl_, wfi[b][:, q_ * W4o:(q_ + 1) * W4o],
                   d_wo[e * NFS + s][:, q_ * W4o:(q_ + 1) * W4o])
                mldo.append((sW[sl_], sW[sl_].n))
            wv(mldo)
            wv(wo_free[b])
            V("tensor_copy", out=wo_h[b][:], in_=wfi[b][:])
            wv(sv)
            mspo = V("scalar_tensor_tensor", out=wo_l[b][:], in0=wo_h[b][:],
                     scalar=-1.0, in1=wfi[b][:], op0=ALU.mult, op1=ALU.add)
            wfi_free[b] = mspo
            return [mspi, mspo]

        def mm1_strip(s, wmark):
            b = s % 2
            bb = 0 if s % 2 == 0 else 4
            wt(wmark[0])
            wt(ei_free)
            out = []
            for ft in range(NFT):
                usebanks(bb + ft)
                first = True
                for wih, eis in ((wi_h[b], ei_h), (wi_h[b], ei_l), (wi_l[b], ei_h)):
                    for it in range(NI):
                        last = (wih is wi_l[b]) and it == NI - 1
                        m1 = (T if last else Tq)(
                            "matmul", BK(bb + ft, 0, C),
                            wih[:, it * FS + ft * 128: it * FS + ft * 128 + 128],
                            eis[:, it * C:(it + 1) * C],
                            start=first, stop=last)
                        first = False
                out.append(m1)
            wi_free[b] = m1
            wil_free[b] = m1
            wih_free[b] = m1
            return out

        wmarks = {0: ld_strip(0), 1: ld_strip(1)}
        mm1m = {0: mm1_strip(0, wmarks[0])}
        for s in range(NFS):
            b = s % 2
            bb = 0 if s % 2 == 0 else 4
            for ft in range(NFT):
                ws(mm1m[s][ft])
                ws(hact_rd[s % 2])
                ws(hact_f_rd[0])
                Sc("activation", out=hact_f[:, ft * C:(ft + 1) * C],
                   in_=BK(bb + ft, 0, C), func=AF.Gelu_apprx_tanh)
                mgl = Sc("activation", out=hact_h[s % 2][:, ft * C:(ft + 1) * C],
                         in_=BK(bb + ft, 0, C), func=AF.Gelu_apprx_tanh)
                freebanks(mgl, bb + ft)
            wv(mgl)
            wv(sv)
            mhl = V("scalar_tensor_tensor", out=hact_l[s % 2][:],
                    in0=hact_h[s % 2][:], scalar=-1.0, in1=hact_f[:],
                    op0=ALU.mult, op1=ALU.add)
            hact_f_rd[0] = mhl
            if s + 1 < NFS:
                mm1m[s + 1] = mm1_strip(s + 1, wmarks[s + 1])
            wt(mhl)
            wt(wmarks[s][1])
            for ct in range(NCT):
                for hh in range(2):
                    b2 = 2 + (ct * 2 + hh) % 2
                    usebanks(b2)
                    first = True
                    for ha, wos in ((hact_h[s % 2], wo_h[b]), (hact_h[s % 2], wo_l[b]),
                                    (hact_l[s % 2], wo_h[b])):
                        for ft in range(NFT):
                            last = (ha is hact_l[s % 2]) and ft == NFT - 1
                            mm2 = (T if (last or (ct == NCT - 1 and hh == 1 and ft == NFT - 1)) else Tq)(
                                "matmul", BK(b2, 0, 512),
                                ha[:, ft * C + ct * 128: ft * C + ct * 128 + 128],
                                wos[:, ft * H + hh * 512: ft * H + hh * 512 + 512],
                                start=(first), stop=last)
                            first = False
                    wv(mm2)
                    wv(sv)
                    if s == 0:
                        meo = V("tensor_copy",
                                out=eo_sb[:, ct * H + hh * 512: ct * H + hh * 512 + 512],
                                in_=BK(b2, 0, 512))
                    else:
                        meo = V("tensor_tensor",
                                out=eo_sb[:, ct * H + hh * 512: ct * H + hh * 512 + 512],
                                in0=eo_sb[:, ct * H + hh * 512: ct * H + hh * 512 + 512],
                                in1=BK(b2, 0, 512), op=ALU.add)
                    freebanks(meo, b2)
            hact_rd[s % 2] = mm2
            wo_free[b] = mm2
            wol_free[b] = mm2
            woh_free[b] = mm2
            if s + 2 < NFS:
                wmarks[s + 2] = ld_strip(s + 2)
        if DBG and e == 0:
            wg(meo)
            st(3, d_dbg_eo, eo_sb[:])
        wv([meo, gate_free, sv])
        for ct in range(NCT):
            V("tensor_scalar", out=eo_sb[:, ct * H:(ct + 1) * H],
              in0=eo_sb[:, ct * H:(ct + 1) * H],
              scalar1=gate_sb[:, ct:ct + 1], scalar2=None, op0=ALU.mult)
        wv(sv)
        V("tensor_copy", out=eo_h[:], in_=eo_sb[:])
        wv(sv)
        meog = V("scalar_tensor_tensor", out=eo_l[:], in0=eo_h[:], scalar=-1.0,
                 in1=eo_sb[:], op0=ALU.mult, op1=ALU.add)
        eo_free = meog
        wv(midbf)

        def gen_dispT(tt):
            b = tt % 2
            wv(disp_free[b])
            wv(sv)
            mdg = V("tensor_scalar", out=dispb[b][:], in0=c_iotaC[:],
                    scalar1=r_prio[:, tt * E + e: tt * E + e + 1], scalar2=None,
                    op0=ALU.is_equal)
            wt(mdg)
            for ct in range(NCT):
                b3 = 6 + ct % 2
                usebanks(b3)
                mtr3 = T("transpose", BK(b3, 0, 128).bitcast(BF16)[:, 0:128],
                         dispb[b][:, ct * 128:(ct + 1) * 128], c_ident_bf[:])
                wv(mtr3)
                wv(dTg_rd[b])
                mdt = V("tensor_copy", out=dispTg[b][:, ct * 128:(ct + 1) * 128],
                        in_=BK(b3, 0, 128).bitcast(BF16)[:, 0:128])
                freebanks(mdt, b3)
            disp_free[b] = mtr3
            return mdt

        mdt_t = {0: gen_dispT(0)}
        for tt in range(NTO):
            b = tt % 2
            if tt + 1 < NTO:
                mdt_t[tt + 1] = gen_dispT(tt + 1)
            wt(mdt_t[tt])
            wt(eo_free)
            for hh in range(2):
                b2 = 2 + (tt * 2 + hh) % 4
                usebanks(b2)
                first = True
                for eos in (eo_h, eo_l):
                    for ct in range(NCT):
                        last = (eos is eo_l) and ct == NCT - 1
                        mcb = (T if last else Tq)(
                            "matmul", BK(b2, 0, 512),
                            dispTg[b][:, ct * 128:(ct + 1) * 128],
                            eos[:, ct * H + hh * 512: ct * H + hh * 512 + 512],
                            start=first, stop=last)
                        first = False
                wv(mcb)
                wv(out_mark)
                out_mark = V("scalar_tensor_tensor",
                             out=out_acc[:, tt * H + hh * 512: tt * H + hh * 512 + 512],
                             in0=BK(b2, 0, 512), scalar=1.0,
                             in1=out_acc[:, tt * H + hh * 512: tt * H + hh * 512 + 512],
                             op0=ALU.mult, op1=ALU.add)
                freebanks(out_mark, b2)
            dTg_rd[b] = mcb
    if DBG:
        wg(out_mark)
        for tt in range(NTO):
            st(3, d_dbg_moe[tt * 128:(tt + 1) * 128, :], out_acc[:, tt * H:(tt + 1) * H])
    em.barrier()
    pmm.close()

    woT_sb = sb("pp_woT", [128, NI * H], st=pm)
    obuf = [sb(f"pp_ob{i}", [128, H], st=pm) for i in range(2)]

    ld(2, woT_sb[:], d_woT)
    mwo2 = (sW[2], sW[2].n)
    ob_free = [None, None]
    for tt in range(NTO):
        b = tt % 2
        wsync(ob_free[b])
        mo = []
        for hp in range(HP):
            mo.append(ld(4 + (hp % 8), obuf[b][:, hp * 128:(hp + 1) * 128],
                         d_ost[hp, :, tt * 128:(tt + 1) * 128]))
        wt(mo)
        wt(mwo2)
        for no in range(2):
            b2 = 4 + (tt * 2 + no) % 2
            usebanks(b2)
            for hp in range(HP):
                mm = (T if hp == HP - 1 else Tq)(
                    "matmul", BK(b2, 0, 512), obuf[b][:, hp * 128:(hp + 1) * 128],
                    woT_sb[:, hp * H + no * 512: hp * H + no * 512 + 512],
                    start=(hp == 0), stop=(hp == HP - 1))
            ob_free[b] = mm
            wv(mm)
            wv(out_mark)
            out_mark = V("scalar_tensor_tensor",
                         out=out_acc[:, tt * H + no * 512: tt * H + no * 512 + 512],
                         in0=BK(b2, 0, 512), scalar=1.0,
                         in1=out_acc[:, tt * H + no * 512: tt * H + no * 512 + 512],
                         op0=ALU.mult, op1=ALU.add)
            freebanks(out_mark, b2)

    borow = sb("pf_borow", [1, H], st=pm)
    bob = sb("pf_bob", [128, H], st=pm)
    xfin = [sb(f"pf_x{i}", [128, H], st=pm) for i in range(2)]
    ld(3, borow[:], d_bo)
    wt((sW[3], sW[3].n))
    mbo = bcast(bob, borow, H, 6)
    xf_free = [None, None]
    for tt in range(NTO):
        b = tt % 2
        wsync(xf_free[b])
        mx = ld(b, xfin[b][:], d_x_own[tt * 128:(tt + 1) * 128, :])
        wv([mx, mbo, out_mark])
        V("tensor_tensor", out=out_acc[:, tt * H:(tt + 1) * H],
          in0=out_acc[:, tt * H:(tt + 1) * H], in1=bob[:], op=ALU.add)
        wv(sv)
        my = V("tensor_tensor", out=out_acc[:, tt * H:(tt + 1) * H],
               in0=out_acc[:, tt * H:(tt + 1) * H], in1=xfin[b][:], op=ALU.add)
        xf_free[b] = my
        wg(my)
        st(6, d_y[tt * 128:(tt + 1) * 128, :], out_acc[:, tt * H:(tt + 1) * H])
    em.wait("gpsimd", sG[6])
    em.barrier()
    pm.close()

    em.run()
    es.close()
    return nc


_CFG = CFG()


def _rope_perm(nh=16, hd=64):
    p = []
    for h in range(nh):
        for j in range(hd // 2):
            p.append(h * hd + j)
            p.append(h * hd + j + hd // 2)
    return np.array(p)


def _pit(m):
    r, c = m.shape
    return np.ascontiguousarray(m.reshape(r // 128, 128, c).transpose(1, 0, 2)
                                .reshape(128, -1))


def _wi3(w, cfg):
    E, H, F = w.shape
    NI, NFS, FS = H // 128, F // cfg.FS, cfg.FS
    return np.ascontiguousarray(w.reshape(E, NI, 128, NFS, FS)
                                .transpose(0, 3, 2, 1, 4).reshape(E * NFS, 128, NI * FS))


def _wo3(w, cfg):
    E, F, H = w.shape
    NFS, FS = F // cfg.FS, cfg.FS
    NFT = FS // 128
    return np.ascontiguousarray(w.reshape(E, NFS, NFT, 128, H)
                                .transpose(0, 1, 3, 2, 4).reshape(E * NFS, 128, NFT * H))


def _phpmaj(m):
    r, c = m.shape
    return np.ascontiguousarray(m.reshape(r // 128, 128, c // 128, 128)
                                .transpose(1, 2, 0, 3).reshape(128, -1))


def kernel(L0_attn_x, L0_attn_wq, L0_attn_bq, L0_attn_wk, L0_attn_bk,
           L0_attn_wv, L0_attn_bv, L0_attn_wo, L0_attn_bo,
           L0_ffn_w_router, L0_ffn_wi, L0_ffn_wo):
    cfg = _CFG
    perm = _rope_perm(cfg.NH, cfg.HD)
    x = np.ascontiguousarray(np.asarray(L0_attn_x, dtype=np.float32))
    Bb, S, H = x.shape
    TO = cfg.TO
    asc = np.ascontiguousarray
    shared = {
        "wqT": _pit(np.asarray(L0_attn_wq, np.float32)[perm].T),
        "wkT": _phpmaj(np.asarray(L0_attn_wk, np.float32)[perm].T),
        "wvT": _phpmaj(np.asarray(L0_attn_wv, np.float32).T),
        "woT": _pit(np.asarray(L0_attn_wo, np.float32).T),
        "bq": asc(np.asarray(L0_attn_bq, np.float32)[perm].reshape(-1, 128).T),
        "bk": asc(np.asarray(L0_attn_bk, np.float32)[perm].reshape(-1, 128).T),
        "bv": asc(np.asarray(L0_attn_bv, np.float32).reshape(1, H)),
        "bo": asc(np.asarray(L0_attn_bo, np.float32).reshape(1, H)),
        "w_router": asc(np.asarray(L0_ffn_w_router, np.float32)),
        "wi": _wi3(np.asarray(L0_ffn_wi, np.float32), cfg),
        "wo_moe": _wo3(np.asarray(L0_ffn_wo, np.float32), cfg),
    }
    in_maps = []
    for core in range(8):
        b, half = core // 2, core % 2
        m = dict(shared)
        m["x_own"] = asc(x[b, half * TO:(half + 1) * TO, :])
        m["x_oth"] = asc(x[b, (1 - half) * TO:(2 - half) * TO, :])
        m["pos_own"] = np.full((128, 1), half * TO, np.float32)
        m["pos_oth"] = np.full((128, 1), (1 - half) * TO, np.float32)
        in_maps.append(m)
    nc = build_nc(cfg)
    res = run_bass_kernel_spmd(nc, in_maps, core_ids=list(range(8)))
    out = np.empty((Bb, S, H), np.float32)
    for core in range(8):
        b, half = core // 2, core % 2
        out[b, half * TO:(half + 1) * TO, :] = res.results[core]["y"]
    return out
